# version 22
# baseline (speedup 1.0000x reference)
import sys
import numpy as np

try:
    import concourse.bass as bass
except Exception:
    sys.path.insert(0, "/opt/trn_rl_repo")
    import concourse.bass as bass
import ml_dtypes
import concourse.mybir as mybir
from concourse.bass_utils import run_bass_kernel_spmd

F32 = mybir.dt.float32
BF16 = mybir.dt.bfloat16
AF = mybir.ActivationFunctionType
ALU = mybir.AluOpType
AX = mybir.AxisListType

CMAGIC = 12582912.0
NEG = -1.0e9
EPS = 1.0e-5
T = 2048
NQT = 16
E = 1024
I3 = 3072
II = 2816
IC = 22
FC = 8
ENGS = ("sync", "gpsimd", "scalar", "vector", "tensor")


class Sem:
    def __init__(self, nc, name):
        self.h = nc.alloc_semaphore(name)
        self.n = 0


class Sched:

    def __init__(self):
        self.q = {k: [] for k in ENGS}

    def op(self, eng, fn, waits=(), inc=None, inc_n=1):
        w = tuple((s, v) for (s, v) in waits if v > 0)
        if inc is not None:
            inc.n += inc_n

        def emit(e, fn=fn, w=w, inc=inc, inc_n=inc_n):
            for s, v in w:
                e.wait_ge(s.h, v)
            r = fn(e)
            if inc is not None:
                r.then_inc(inc.h, inc_n)
        self.q[eng].append(emit)
        return inc.n if inc is not None else None

    def wait(self, eng, sem, val):
        if val > 0:
            self.q[eng].append(lambda e, s=sem, v=val: e.wait_ge(s.h, v))

    def play(self, block):
        for name in ENGS:
            ops = self.q[name]
            if ops:
                getattr(block, name)(lambda eng, ops=ops: [f(eng) for f in ops])
        self.q = {k: [] for k in ENGS}


_DEBUG = {}


def build_nc():
    nc = bass.Bass("TRN2", target_bir_lowering=False, detect_race_conditions=False)

    dx = nc.dram_tensor("x_main", [T, E], F32, kind="ExternalInput")
    dxh = nc.dram_tensor("x_halo", [128, E], F32, kind="ExternalInput")
    dmf = nc.dram_tensor("mask_first", [128, 256], BF16, kind="ExternalInput")
    dmb = nc.dram_tensor("mask_band", [128, 256], BF16, kind="ExternalInput")
    dwin = nc.dram_tensor("w_inT", [E, I3], F32, kind="ExternalInput")
    dwout = nc.dram_tensor("w_outT", [E, E], F32, kind="ExternalInput")
    dbpm = nc.dram_tensor("b_pm", [128, 24], F32, kind="ExternalInput")
    dbv = nc.dram_tensor("b_v_row", [1, E], F32, kind="ExternalInput")
    dbo = nc.dram_tensor("b_out_row", [1, E], F32, kind="ExternalInput")
    dwg = nc.dram_tensor("w_gate", [II, E], F32, kind="ExternalInput")
    dwu = nc.dram_tensor("w_up", [II, E], F32, kind="ExternalInput")
    dwd = nc.dram_tensor("w_down", [E, II], F32, kind="ExternalInput")
    dcx = nc.dram_tensor("clip_x", [1, 1], F32, kind="ExternalInput")
    dcd = nc.dram_tensor("clip_down", [1, 1], F32, kind="ExternalInput")
    dib = nc.dram_tensor("ident_bf", [128, 128], BF16, kind="ExternalInput")
    dif = nc.dram_tensor("ident_f32", [128, 128], F32, kind="ExternalInput")
    donb = nc.dram_tensor("ones_bf", [1, 128], BF16, kind="ExternalInput")
    donf = nc.dram_tensor("ones_f32", [1, 128], F32, kind="ExternalInput")
    dy = nc.dram_tensor("y", [T, E], F32, kind="ExternalOutput")
    dbg_sc = nc.dram_tensor("dbg_sc", [1, 8], F32, kind="ExternalOutput") if _DEBUG.get("scales") else None
    dbg_xq = nc.dram_tensor("dbg_xq", [128, 8 * 2048], F32, kind="ExternalOutput") if _DEBUG.get("stop") == "xq" else None
    dbg_g = nc.dram_tensor("dbg_g", [128, IC * 2048], F32, kind="ExternalOutput") if _DEBUG.get("stop") == "g" else None
    if _DEBUG.get("stop") == "kv0":
        dbg_x = nc.dram_tensor("dbg_x", [128, 4096], F32, kind="ExternalOutput")
        dbg_ln1f = nc.dram_tensor("dbg_ln1f", [128, 8192], F32, kind="ExternalOutput")
        dbg_qf = nc.dram_tensor("dbg_qf", [128, 4096], F32, kind="ExternalOutput")
        dbg_kf = nc.dram_tensor("dbg_kf", [128, 8192], F32, kind="ExternalOutput")
        dbg_vt = nc.dram_tensor("dbg_vt", [128, 8192], F32, kind="ExternalOutput")
        dbg_st = nc.dram_tensor("dbg_st", [128, 8], F32, kind="ExternalOutput")
    cc_in_x = nc.dram_tensor("cc_in_x", [1, 128], F32)
    cc_out_x = nc.dram_tensor("cc_out_x", [1, 128], F32)
    cc_in_g = nc.dram_tensor("cc_in_g", [1, 128], F32)
    cc_out_g = nc.dram_tensor("cc_out_g", [1, 128], F32)

    SX = Sem(nc, "SX")
    SG = Sem(nc, "SG")
    SCC = Sem(nc, "SCC")
    ST = Sem(nc, "ST")
    SV = Sem(nc, "SV")
    SS = Sem(nc, "SS")
    SYP = Sem(nc, "SYP")
    SYW = Sem(nc, "SYW")
    SGW = Sem(nc, "SGW")
    SGD = Sem(nc, "SGD")
    SDP = [Sem(nc, "SDP%d" % i) for i in range(3)]
    SWP = [Sem(nc, "SWP%d" % i) for i in range(4)]
    SGA = Sem(nc, "SGA")

    from contextlib import ExitStack
    es0 = ExitStack()

    def sb0(name, shape, dtype):
        return es0.enter_context(nc.sbuf_tensor(name, list(shape), dtype))

    mfirst = sb0("mfirst", [128, 256], BF16)
    mband = sb0("mband", [128, 256], BF16)
    identb = sb0("identb", [128, 128], BF16)
    identf = sb0("identf", [128, 128], F32)
    onesb = sb0("onesb", [1, 128], BF16)
    onesf = sb0("onesf", [1, 128], F32)
    bsb = sb0("bsb", [128, 24], F32)
    bqs = sb0("bqs", [128, 8], F32)
    bvrow = sb0("bvrow", [1, E], BF16)
    borow = sb0("borow", [1, E], BF16)
    clipx = sb0("clipx", [1, 1], F32)
    clipd = sb0("clipd", [1, 1], F32)
    scr = sb0("scr", [128, 1024], F32)
    m_all = sb0("m_all", [128, NQT], F32)
    rstd_all = sb0("rstd_all", [128, NQT], F32)
    lnstat = sb0("lnstat", [128, 32], F32)
    qstat = sb0("qstat", [128, 16], F32)
    tmpv = sb0("tmpv", [128, 4], F32)
    absx = sb0("absx", [128, 1], F32)
    absg = sb0("absg", [128, 1], F32)
    sigx = sb0("sigx", [1, 1], F32)
    sigd = sb0("sigd", [1, 1], F32)
    t11 = sb0("t11", [1, 4], F32)
    rxv = sb0("rxv", [1, 1], F32)
    rhv = sb0("rhv", [1, 1], F32)
    sxv = sb0("sxv", [1, 1], F32)
    shv = sb0("shv", [1, 1], F32)
    rxb = sb0("rxb", [128, 1], F32)
    sxb = sb0("sxb", [128, 1], F32)
    rhb = sb0("rhb", [128, 1], F32)
    shb = sb0("shb", [128, 1], F32)
    ccrow = sb0("ccrow", [1, 128], F32)
    ccrow2 = sb0("ccrow2", [1, 128], F32)
    vjunk = sb0("vjunk", [128, 1], F32)
    cEPS = sb0("cEPS", [128, 1], F32)
    cMAG = sb0("cMAG", [128, 1], F32)
    cNMG = sb0("cNMG", [128, 1], F32)
    cN05 = sb0("cN05", [128, 1], F32)
    cN1 = sb0("cN1", [128, 1], F32)
    c18 = sb0("c18", [128, 1], F32)

    pq = es0.enter_context(nc.psum_tensor("pq", [128, 512], F32))
    pk = es0.enter_context(nc.psum_tensor("pk", [128, 512], F32))
    pss = es0.enter_context(nc.psum_tensor("pss", [128, 512], F32))
    pst = es0.enter_context(nc.psum_tensor("pst", [128, 512], F32))
    psl = es0.enter_context(nc.psum_tensor("psl", [128, 512], F32))
    pso = es0.enter_context(nc.psum_tensor("pso", [128, 128], F32))
    psp = es0.enter_context(nc.psum_tensor("psp", [128, 1024], F32))

    sch = Sched()
    psprod = {}
    pscons = {}

    def ps_w(name):
        return [pscons[name]] if name in pscons else []

    def vgap():
        sch.op("vector", lambda e: e.tensor_scalar(
            vjunk[:, :], cEPS[:, :], 1.0, None, ALU.mult))

    def ckpt(eng, dma_sem, prog_sem):
        sch.wait(eng, dma_sem, dma_sem.n)
        return sch.op(eng, lambda e: e.nop(), inc=prog_sem)

    esb = ExitStack()

    def sbB(name, shape, dtype):
        return esb.enter_context(nc.sbuf_tensor(name, list(shape), dtype))

    xring = sbB("xring", [128, 6 * 1024], F32)
    xh = sbB("xh", [128, 1024], F32)
    ln1b = sbB("ln1b", [128, 2 * 1024], BF16)
    ln1f = sbB("ln1f", [128, 2 * 4096], BF16)
    qf2 = sbB("qf2", [128, 2 * 4096], BF16)
    kf2 = sbB("kf2", [128, 8 * 1024], BF16)
    vtb = sbB("vtb", [128, 8 * 1024], BF16)
    winT = sbB("winT", [128, 8 * I3], BF16)
    woutT = sbB("woutT", [128, 8 * 1024], BF16)
    probs2 = sbB("probs2", [128, 2 * 512], BF16)
    probsT2 = sbB("probsT2", [128, 2 * 512], BF16)
    diag4 = sbB("diag4", [128, 4 * 128], BF16)
    nm2 = sbB("nm2", [128, 2], F32)
    sums4 = sbB("sums4", [128, 4], F32)
    rc4 = sbB("rc4", [128, 4], F32)
    ao2 = sbB("ao2", [128, 2 * 1024], BF16)
    x1t = sbB("x1t", [128, 2 * 1024], F32)
    x1s = sbB("x1s", [128, 4], F32)

    sch.op("vector", lambda e: e.memset(absx[:, :], 0.0))
    sch.op("vector", lambda e: e.memset(absg[:, :], 0.0))
    sch.op("vector", lambda e: e.memset(cEPS[:, :], EPS))
    sch.op("vector", lambda e: e.memset(cMAG[:, :], CMAGIC))
    sch.op("vector", lambda e: e.memset(cNMG[:, :], -CMAGIC))
    sch.op("vector", lambda e: e.memset(cN05[:, :], -0.5))
    sch.op("vector", lambda e: e.memset(cN1[:, :], -1.0))
    sv_consts = sch.op("vector", lambda e: e.memset(c18[:, :], 0.125), inc=SV)

    for dst, src in [(mfirst, dmf), (mband, dmb), (identb, dib), (identf, dif),
                     (onesb, donb), (onesf, donf), (bsb, dbpm), (clipx, dcx),
                     (clipd, dcd)]:
        sch.op("sync", lambda e, dst=dst, src=src: e.dma_start(
            out=dst[:, :], in_=src[:, :]), inc=SX, inc_n=16)
    sx_consts = SX.n

    sch.op("gpsimd", lambda e: e.dma_start(out=bvrow[:, :], in_=dbv[:, :]), inc=SG, inc_n=16)
    sch.op("gpsimd", lambda e: e.dma_start(out=borow[:, :], in_=dbo[:, :]), inc=SG, inc_n=16)
    for fc in range(FC):
        sch.op("gpsimd", lambda e, fc=fc: e.dma_start(
            out=winT[:, fc * I3:(fc + 1) * I3],
            in_=dwin[fc * 128:(fc + 1) * 128, :]), inc=SG, inc_n=16)
    sg_win = SG.n
    for fc in range(FC):
        sch.op("gpsimd", lambda e, fc=fc: e.dma_start(
            out=woutT[:, fc * 1024:(fc + 1) * 1024],
            in_=dwout[fc * 128:(fc + 1) * 128, :]), inc=SG, inc_n=16)
    sg_weights = SG.n

    from concourse.hw_specs import get_activation_tables
    tables = list(get_activation_tables(nc.m.arch).keys())
    SET_EXPLN = tables.index("natural_log_exp_and_others")
    SET_SILU = tables.index("silu_and_others")

    def load_table(idx):
        def f(e):
            return e.add_instruction(mybir.InstLoadActFuncSet(
                name=nc.get_next_instruction_name(), act_func_set_id=idx,
                ins=[], outs=[]))
        return f
    sch.op("scalar", load_table(SET_EXPLN))

    sv_bqs = sch.op("vector", lambda e: e.tensor_scalar(
        bqs[:, :], bsb[:, 0:8], 0.125, None, ALU.mult),
        waits=[(SYP, 1)], inc=SV)

    sch.op("scalar", lambda e: e.activation(
        t11[0:1, 0:1], clipx[:, :], AF.Exp, bias=0.0, scale=cN1[0:1, 0:1]),
        waits=[(SYP, 1), (SV, sv_consts)])
    ss_e2 = sch.op("scalar", lambda e: e.activation(
        t11[0:1, 1:2], clipd[:, :], AF.Exp, bias=0.0, scale=cN1[0:1, 0:1]), inc=SS)
    sch.op("vector", lambda e: e.tensor_scalar(
        t11[0:1, 0:2], t11[0:1, 0:2], 1.0, None, ALU.add), waits=[(SS, ss_e2)])
    vgap()
    sch.op("vector", lambda e: e.reciprocal(sigx[:, :], t11[0:1, 0:1]))
    sv_sig = sch.op("vector", lambda e: e.reciprocal(sigd[:, :], t11[0:1, 1:2]), inc=SV)

    x_load_sx = {}
    x_last_read = {}
    y_write_sx = {}
    kt_last_read = {}
    lnb_lastT = {}
    kv_write_ss = {}

    def xsl(mt, lo, n):
        return xring[:, (mt % 6) * 1024 + lo:(mt % 6) * 1024 + lo + n]

    def kcol(kt):
        return ((((kt - 1) // 4) % 2)) * 512 + ((kt - 1) % 4) * 128

    def vcol(kt):
        return ((((kt - 1) // 4) % 2)) * 4096 + ((kt - 1) % 4) * 1024

    def ln1_stats(mt, src, halo=False):
        p4 = 0 if halo else (mt % 4)
        c0 = p4 * 8
        w = [(SYP, 1 if halo else (mt // 4) + 1)]
        sch.op("vector", lambda e: e.tensor_reduce(
            lnstat[:, c0:c0 + 1], src(0, 1024), AX.X, ALU.add), waits=w)
        sch.op("vector", lambda e: e.scalar_tensor_tensor(
            scr[:, :], src(0, 1024), 1.0, src(0, 1024), ALU.mult, ALU.mult,
            accum_out=lnstat[:, c0 + 2:c0 + 3]))
        sch.op("vector", lambda e: e.tensor_scalar(
            lnstat[:, c0 + 1:c0 + 2], lnstat[:, c0:c0 + 1], 1.0 / 1024.0, None, ALU.mult))
        sch.op("vector", lambda e: e.tensor_scalar(
            lnstat[:, c0 + 3:c0 + 4], lnstat[:, c0 + 2:c0 + 3], 1.0 / 1024.0, None, ALU.mult))
        sch.op("vector", lambda e: e.tensor_tensor(
            lnstat[:, c0 + 4:c0 + 5], lnstat[:, c0 + 1:c0 + 2],
            lnstat[:, c0 + 1:c0 + 2], ALU.mult))
        vgap()
        svv = sch.op("vector", lambda e: e.tensor_tensor(
            lnstat[:, c0 + 5:c0 + 6], lnstat[:, c0 + 3:c0 + 4],
            lnstat[:, c0 + 4:c0 + 5], ALU.subtract), inc=SV)
        ssl = sch.op("scalar", lambda e: e.activation(
            lnstat[:, c0 + 6:c0 + 7], lnstat[:, c0 + 5:c0 + 6], AF.Ln,
            bias=cEPS[:, :], scale=1.0), waits=[(SV, svv)], inc=SS)
        ssr = sch.op("scalar", lambda e: e.activation(
            lnstat[:, c0 + 7:c0 + 8], lnstat[:, c0 + 6:c0 + 7], AF.Exp,
            bias=0.0, scale=cN05[:, :]), waits=[(SS, ssl)], inc=SS)
        return c0, ssr

    def ln1_norm(mt, src, c0, ssr, halo=False):
        par = 0 if halo else (mt % 2)
        w = [(SS, ssr)]
        if par in lnb_lastT:
            w.append((ST, lnb_lastT[par]))
        return par, sch.op("vector", lambda e, par=par: e.tensor_scalar(
            ln1b[:, par * 1024:(par + 1) * 1024], src(0, 1024),
            lnstat[:, c0 + 1:c0 + 2], lnstat[:, c0 + 7:c0 + 8],
            ALU.subtract, ALU.mult), waits=w, inc=SV)

    def ln1_transpose(par, dstbase, svn):
        w = ps_w("psp") + [(SV, svn), (SG, sg_win), (SYP, 1)]
        stv = None
        for half in range(2):
            for c4 in range(4):
                c = half * 4 + c4
                stv = sch.op("tensor", lambda e, half=half, c4=c4, c=c, par=par: e.matmul(
                    psp[:, half * 512 + c4 * 128:half * 512 + (c4 + 1) * 128],
                    ln1b[:, par * 1024 + c * 128:par * 1024 + (c + 1) * 128],
                    identb[:, :], start=True, stop=True),
                    waits=w if (half == 0 and c4 == 0) else [],
                    inc=ST if (half == 1 and c4 == 3) else None)
        psprod["psp"] = (ST, stv)
        lnb_lastT[par] = stv
        last_ss = None
        for half in range(2):
            def ev(e, dstbase=dstbase, half=half):
                src = bass.AP(psp, half * 512, [[1024, 128], [128, 4], [1, 128]])
                dst = bass.AP(ln1f, dstbase + half * 4 * 512,
                              [[8192, 128], [512, 4], [1, 128]])
                return e.copy(dst, src)
            last_ss = sch.op("scalar", ev,
                             waits=[psprod["psp"]] if half == 0 else [], inc=SS)
        pscons["psp"] = (SS, last_ss)
        return last_ss

    sch.op("sync", lambda e: e.dma_start(out=xh[:, :], in_=dxh[:, :]), inc=SX, inc_n=16)
    x_load_sx["halo"] = SX.n
    hsrc = lambda lo, n: xh[:, lo:lo + n]
    c0h, ssrh = ln1_stats(0, hsrc, halo=True)
    _, svh = ln1_norm(0, hsrc, c0h, ssrh, halo=True)
    ss_halo_T = ln1_transpose(0, 4096 + 3 * 128, svh)

    for oc in range(FC):
        w = ps_w("pk") + [(SS, ss_halo_T)]
        for fc in range(FC):
            stv = sch.op("tensor", lambda e, oc=oc, fc=fc: e.matmul(
                pk[:, 0:128],
                winT[:, fc * I3 + 1024 + oc * 128:fc * I3 + 1024 + (oc + 1) * 128],
                ln1f[:, 4096 + fc * 512 + 384:4096 + fc * 512 + 512],
                start=(fc == 0), stop=(fc == FC - 1)),
                waits=w if fc == 0 else [], inc=ST if fc == FC - 1 else None)
        psprod["pk"] = (ST, stv)
        ssv = sch.op("scalar", lambda e, oc=oc: e.activation(
            kf2[:, oc * 1024 + 512 + 384:oc * 1024 + 512 + 512], pk[:, 0:128],
            AF.Identity, bias=bsb[:, 8 + oc:9 + oc], scale=1.0),
            waits=[psprod["pk"]], inc=SS)
        pscons["pk"] = (SS, ssv)
    w = ps_w("psp")
    for n in range(2):
        sch.op("tensor", lambda e, n=n: e.matmul(
            psp[:, n * 512:(n + 1) * 512], onesb[:, :],
            bvrow[0:1, n * 512:(n + 1) * 512], start=True, stop=False),
            waits=w if n == 0 else [])
        for fc in range(FC):
            stv = sch.op("tensor", lambda e, n=n, fc=fc: e.matmul(
                psp[:, n * 512:(n + 1) * 512],
                ln1f[:, 4096 + fc * 512 + 384:4096 + fc * 512 + 512],
                winT[:, fc * I3 + 2048 + n * 512:fc * I3 + 2048 + (n + 1) * 512],
                start=False, stop=(fc == FC - 1)), inc=ST if fc == FC - 1 else None)
    psprod["psp"] = (ST, stv)
    ssv = sch.op("scalar", lambda e: e.copy(
        vtb[:, 4096 + 3 * 1024:4096 + 4 * 1024], psp[:, :]),
        waits=[psprod["psp"]], inc=SS)
    pscons["psp"] = (SS, ssv)

    def produce_pieces(G):
        gb = G % 2
        stats = {}
        lss = [None]
        kvr = {}

        def xload(mt, ck=False):
            def th(mt=mt, ck=ck):
                w = [x_last_read[mt - 6]] if (mt - 6) in x_last_read else []
                sch.op("sync", lambda e, mt=mt: e.dma_start(
                    out=xsl(mt, 0, 1024), in_=dx[mt * 128:(mt + 1) * 128, :]),
                    waits=w, inc=SX, inc_n=16)
                x_load_sx[mt] = SX.n
                if ck:
                    ckpt("sync", SX, SYP)
            return th

        def stat_p(tt):
            def th(tt=tt):
                mt = 4 * G + tt
                src = lambda lo, n, mt=mt: xsl(mt, lo, n)
                stats[mt] = ln1_stats(mt, src)
            return th

        def norm_p(tt):
            def th(tt=tt):
                mt = 4 * G + tt
                src = lambda lo, n, mt=mt: xsl(mt, lo, n)
                c0, ssr = stats[mt]
                par, svn = ln1_norm(mt, src, c0, ssr)
                lss[0] = ln1_transpose(par, gb * 4096 + tt * 128, svn)
            return th

        def vproj_p(tt):
            def th(tt=tt):
                mt = 4 * G + tt
                kt = mt + 1
                wv = ps_w("psp") + [(SS, lss[0])]
                if (kt - 8) in kt_last_read:
                    wv.append((ST, kt_last_read[kt - 8]))
                for n in range(2):
                    sch.op("tensor", lambda e, n=n: e.matmul(
                        psp[:, n * 512:(n + 1) * 512], onesb[:, :],
                        bvrow[0:1, n * 512:(n + 1) * 512], start=True, stop=False),
                        waits=wv if n == 0 else [])
                    for fc in range(FC):
                        stv = sch.op("tensor", lambda e, n=n, fc=fc, gb=gb, tt=tt: e.matmul(
                            psp[:, n * 512:(n + 1) * 512],
                            ln1f[:, gb * 4096 + fc * 512 + tt * 128:gb * 4096 + fc * 512 + (tt + 1) * 128],
                            winT[:, fc * I3 + 2048 + n * 512:fc * I3 + 2048 + (n + 1) * 512],
                            start=False, stop=(fc == FC - 1)),
                            inc=ST if fc == FC - 1 else None)
                psprod["psp"] = (ST, stv)
                ssv = sch.op("scalar", lambda e, kt=kt: e.copy(
                    vtb[:, vcol(kt):vcol(kt) + 1024], psp[:, :]),
                    waits=[psprod["psp"]], inc=SS)
                pscons["psp"] = (SS, ssv)
            return th

        def qk_p(oc):
            def th(oc=oc):
                kro = [kt_last_read.get(k, 0) for k in range(4 * G - 7, 4 * G - 3)]
                k_reuse = max(kro) if kro else 0
                wq = ps_w("psp") + [(SS, lss[0])]
                for fc in range(FC):
                    stv = sch.op("tensor", lambda e, oc=oc, fc=fc, gb=gb: e.matmul(
                        psp[:, 0:512],
                        winT[:, fc * I3 + oc * 128:fc * I3 + (oc + 1) * 128],
                        ln1f[:, gb * 4096 + fc * 512:gb * 4096 + (fc + 1) * 512],
                        start=(fc == 0), stop=(fc == FC - 1)),
                        waits=wq if fc == 0 else [], inc=ST if fc == FC - 1 else None)
                psprod["psp"] = (ST, stv)
                ssq = sch.op("scalar", lambda e, oc=oc, gb=gb: e.activation(
                    qf2[:, gb * 4096 + oc * 512:gb * 4096 + (oc + 1) * 512], psp[:, 0:512],
                    AF.Identity, bias=bqs[:, oc:oc + 1], scale=c18[:, :]),
                    waits=[psprod["psp"], (SV, sv_bqs)], inc=SS)
                pscons["psp"] = (SS, ssq)
                wk = ps_w("pk")
                if k_reuse:
                    wk.append((ST, k_reuse))
                for fc in range(FC):
                    stv = sch.op("tensor", lambda e, oc=oc, fc=fc, gb=gb: e.matmul(
                        pk[:, :],
                        winT[:, fc * I3 + 1024 + oc * 128:fc * I3 + 1024 + (oc + 1) * 128],
                        ln1f[:, gb * 4096 + fc * 512:gb * 4096 + (fc + 1) * 512],
                        start=(fc == 0), stop=(fc == FC - 1)),
                        waits=wk if fc == 0 else [], inc=ST if fc == FC - 1 else None)
                psprod["pk"] = (ST, stv)
                ssk = sch.op("scalar", lambda e, oc=oc, gb=gb: e.activation(
                    kf2[:, oc * 1024 + gb * 512:oc * 1024 + gb * 512 + 512], pk[:, :],
                    AF.Identity, bias=bsb[:, 8 + oc:9 + oc], scale=1.0),
                    waits=[psprod["pk"]], inc=SS)
                pscons["pk"] = (SS, ssk)
                kvr[oc] = SS.n
            return th

        P = [(1, xload(4 * G)), (1, xload(4 * G + 1)),
             (11, xload(4 * G + 2)), (19, xload(4 * G + 3, ck=True)),
             (20, stat_p(0)), (20, stat_p(1)), (21, stat_p(2)), (21, stat_p(3))]
        for tt in range(4):
            P.append((22 + tt, norm_p(tt)))
        for tt in range(4):
            P.append((26 + tt, vproj_p(tt)))
        for oc in range(FC):
            P.append((26 + oc, qk_p(oc)))
        P.sort(key=lambda t: t[0])
        return P, kvr

    P0, cur_kvr = produce_pieces(0)
    for _, th in P0:
        th()
    sskv_group = SS.n

    for G in range(4):
        gb = G % 2

        if _DEBUG.get("stop") == "kv0" and G == 0:
            sch.wait("sync", SS, sskv_group)
            for tt4 in range(4):
                sch.op("sync", lambda e, tt4=tt4: e.dma_start(
                    out=dbg_x[:, tt4 * 1024:(tt4 + 1) * 1024], in_=xsl(tt4, 0, 1024)),
                    inc=SX, inc_n=16)
            sch.wait("gpsimd", SS, sskv_group)
            sch.op("gpsimd", lambda e: e.dma_start(out=dbg_ln1f[:, :], in_=ln1f[:, :]), inc=SG, inc_n=16)
            sch.op("gpsimd", lambda e: e.dma_start(out=dbg_qf[:, :], in_=qf2[:, 0:4096]), inc=SG, inc_n=16)
            sch.op("gpsimd", lambda e: e.dma_start(out=dbg_kf[:, :], in_=kf2[:, :]), inc=SG, inc_n=16)
            sch.op("gpsimd", lambda e: e.dma_start(out=dbg_vt[:, :], in_=vtb[:, :]), inc=SG, inc_n=16)
            sch.op("sync", lambda e: e.dma_start(out=dbg_st[:, :], in_=lnstat[:, 0:8]), inc=SX, inc_n=16)
            sch.wait("sync", SX, SX.n)
            sch.wait("gpsimd", SG, SG.n)
            with nc.Block() as block:
                sch.play(block)
            esb.close()
            es0.close()
            return nc

        def stageA(u):
            qt = 4 * G + u // 8
            tt = u // 8
            hp = u % 8
            sc_ps, sc_nm = (pss, "pss") if u % 2 == 0 else (pq, "pq")
            mask_t = mfirst if qt == 0 else mband
            w0 = ps_w(sc_nm) + [(SS, cur_kvr[hp])]
            stv = None
            for hl in range(2):
                sch.op("tensor", lambda e, sc_ps=sc_ps, hl=hl, mask_t=mask_t: e.matmul(
                    sc_ps[:, hl * 256:hl * 256 + 256], identb[:, :], mask_t[:, :],
                    start=True, stop=False), waits=w0 if hl == 0 else [])
                col0 = hp * 1024 + kcol(qt)
                col1 = hp * 1024 + kcol(qt + 1)
                def qk(e, sc_ps=sc_ps, hl=hl, col0=col0, col1=col1, hp=hp, tt=tt, gb=gb):
                    rhs = bass.AP(kf2, hl * 64 * 8192 + col0,
                                  [[8192, 64], [col1 - col0, 2], [1, 128]])
                    return e.matmul(
                        sc_ps[:, hl * 256:hl * 256 + 256],
                        qf2[hl * 64:(hl + 1) * 64,
                            gb * 4096 + hp * 512 + tt * 128:gb * 4096 + hp * 512 + (tt + 1) * 128],
                        rhs, start=False, stop=True)
                stv = sch.op("tensor", qk, inc=ST if hl == 1 else None)
            kt_last_read[qt] = ST.n
            kt_last_read[qt + 1] = ST.n
            psprod[sc_nm] = (ST, stv)
            return (ST, stv)

        def stageV1(u, pa):
            sc_ps = pss if u % 2 == 0 else pq
            return sch.op("vector", lambda e, sc_ps=sc_ps, u=u % 2: e.reduce_max(
                nm2[:, u:u + 1], sc_ps[:, :], axis=AX.X, negate=True),
                waits=[pa], inc=SV)

        def stageS1(u, v1):
            sc_ps, sc_nm = (pss, "pss") if u % 2 == 0 else (pq, "pq")
            ssx = None
            for hl in range(2):
                ssx = sch.op("scalar", lambda e, sc_ps=sc_ps, hl=hl, u=u % 2: e.activation(
                    probs2[:, u * 512 + hl * 256:u * 512 + (hl + 1) * 256],
                    sc_ps[:, hl * 256:(hl + 1) * 256], AF.Exp,
                    bias=nm2[:, u:u + 1], scale=1.0,
                    accum_out=sums4[:, u * 2 + hl:u * 2 + hl + 1]),
                    waits=[(SV, v1)] if hl == 0 else [], inc=SS)
            pscons[sc_nm] = (SS, ssx)
            return ssx

        def stageV2(u, s1):
            sch.op("vector", lambda e, u=u % 2: e.reciprocal(
                rc4[:, u * 2:u * 2 + 2], sums4[:, u * 2:u * 2 + 2]),
                waits=[(SS, s1)])
            vgap()
            svd = None
            for hl in range(2):
                svd = sch.op("vector", lambda e, u=u % 2, hl=hl: e.tensor_scalar(
                    diag4[:, (u * 2 + hl) * 128:(u * 2 + hl + 1) * 128],
                    identb[:, :], rc4[:, u * 2 + hl:u * 2 + hl + 1], None, ALU.mult),
                    inc=SV)
            return svd

        def stageB(u, v2, s1):
            pt_ps, pt_nm = (pst, "pst") if u % 2 == 0 else (pk, "pk")
            wt = ps_w(pt_nm) + [(SV, v2), (SS, s1)]
            stv = None
            for j in range(4):
                hl, kc = j // 2, j % 2
                stv = sch.op("tensor",
                    lambda e, pt_ps=pt_ps, j=j, hl=hl, kc=kc, u=u % 2: e.matmul(
                        pt_ps[:, j * 128:(j + 1) * 128],
                        probs2[:, u * 512 + hl * 256 + kc * 128:u * 512 + hl * 256 + (kc + 1) * 128],
                        diag4[:, (u * 2 + hl) * 128:(u * 2 + hl + 1) * 128],
                        start=True, stop=True), waits=wt if j == 0 else [],
                    inc=ST if j == 3 else None)
            psprod[pt_nm] = (ST, stv)
            return (ST, stv)

        def stageS2(u, pb):
            pt_ps, pt_nm = (pst, "pst") if u % 2 == 0 else (pk, "pk")
            sst = sch.op("scalar", lambda e, pt_ps=pt_ps, u=u % 2: e.copy(
                probsT2[:, u * 512:(u + 1) * 512], pt_ps[:, :]),
                waits=[pb], inc=SS)
            pscons[pt_nm] = (SS, sst)
            return (SS, sst)

        def stageC(u, s2):
            qt = 4 * G + u // 8
            hp = u % 8
            po_nm = "pso" if u % 2 == 0 else "psl"
            po_t = pso if u % 2 == 0 else psl
            wo = ps_w(po_nm) + [s2]
            stv = None
            for hl in range(2):
                for kc in range(2):
                    kt = qt + kc
                    h = hp * 2 + hl
                    stv = sch.op("tensor",
                        lambda e, po_t=po_t, hl=hl, kc=kc, kt=kt, h=h, u=u % 2: e.matmul(
                            po_t[hl * 64:(hl + 1) * 64, 0:128],
                            vtb[:, vcol(kt) + h * 64:vcol(kt) + (h + 1) * 64],
                            probsT2[:, u * 512 + (hl * 2 + kc) * 128:u * 512 + (hl * 2 + kc + 1) * 128],
                            start=(kc == 0), stop=(kc == 1)),
                        waits=wo if (hl == 0 and kc == 0) else [],
                        inc=ST if (hl == 1 and kc == 1) else None)
                    kt_last_read[kt] = ST.n
            psprod[po_nm] = (ST, stv)
            return (ST, stv)

        def stageS3(u, pc):
            qt = 4 * G + u // 8
            hp = u % 8
            po_t = pso if u % 2 == 0 else psl
            po_nm = "pso" if u % 2 == 0 else "psl"
            sso = sch.op("scalar", lambda e, po_t=po_t, qt=qt, hp=hp: e.copy(
                ao2[:, (qt % 2) * 1024 + hp * 128:(qt % 2) * 1024 + (hp + 1) * 128],
                po_t[0:128, 0:128]), waits=[pc], inc=SS)
            pscons[po_nm] = (SS, sso)
            return sso

        def finish_qt(qt):
            wv = ps_w("psp") + [(SS, SS.n), (SG, sg_weights)]
            for n in range(2):
                sch.op("tensor", lambda e, n=n: e.matmul(
                    psp[:, n * 512:(n + 1) * 512], onesb[:, :],
                    borow[0:1, n * 512:(n + 1) * 512], start=True, stop=False),
                    waits=wv if n == 0 else [])
                for fc in range(FC):
                    stv = sch.op("tensor", lambda e, n=n, fc=fc, qt=qt: e.matmul(
                        psp[:, n * 512:(n + 1) * 512],
                        ao2[:, (qt % 2) * 1024 + fc * 128:(qt % 2) * 1024 + (fc + 1) * 128],
                        woutT[:, fc * 1024 + n * 512:fc * 1024 + (n + 1) * 512],
                        start=False, stop=(fc == FC - 1)), inc=ST if fc == FC - 1 else None)
            psprod["psp"] = (ST, stv)
            wstt = [psprod["psp"]]
            if qt >= 2:
                wstt.append((SYW, qt - 1))
            svx = None
            for n in range(2):
                svx = sch.op("vector", lambda e, n=n, qt=qt: e.scalar_tensor_tensor(
                    x1t[:, (qt % 2) * 1024 + n * 512:(qt % 2) * 1024 + (n + 1) * 512],
                    psp[:, n * 512:(n + 1) * 512], 1.0,
                    xsl(qt, n * 512, 512), ALU.mult, ALU.add,
                    accum_out=x1s[:, n:n + 1]),
                    waits=wstt if n == 0 else [], inc=SV)
            pscons["psp"] = (SV, svx)
            x_last_read[qt] = (SV, svx)
            if qt >= 1:
                sch.wait("sync", SX, y_write_sx[qt - 1])
                sch.op("sync", lambda e: e.nop(), inc=SYW)
            sch.op("sync", lambda e, qt=qt: e.dma_start(
                out=dy[qt * 128:(qt + 1) * 128, :],
                in_=x1t[:, (qt % 2) * 1024:(qt % 2 + 1) * 1024]),
                waits=[(SV, svx)], inc=SX, inc_n=16)
            y_write_sx[qt] = SX.n

        def ln2_stats(qt):
            q0 = (qt % 2) * 8
            vgap()
            sch.op("vector", lambda e, q0=q0: e.tensor_tensor(
                qstat[:, q0:q0 + 1], x1s[:, 0:1], x1s[:, 1:2], ALU.add))
            sch.op("vector", lambda e, qt=qt, q0=q0: e.scalar_tensor_tensor(
                scr[:, :], x1t[:, (qt % 2) * 1024:(qt % 2 + 1) * 1024], 1.0,
                x1t[:, (qt % 2) * 1024:(qt % 2 + 1) * 1024],
                ALU.mult, ALU.mult, accum_out=qstat[:, q0 + 1:q0 + 2]))
            sch.op("vector", lambda e, qt=qt, q0=q0: e.tensor_scalar(
                m_all[:, qt:qt + 1], qstat[:, q0:q0 + 1], 1.0 / 1024.0, None, ALU.mult))
            sch.op("vector", lambda e, q0=q0: e.tensor_scalar(
                qstat[:, q0 + 2:q0 + 3], qstat[:, q0 + 1:q0 + 2], 1.0 / 1024.0, None, ALU.mult))
            sch.op("vector", lambda e, qt=qt, q0=q0: e.tensor_tensor(
                qstat[:, q0 + 3:q0 + 4], m_all[:, qt:qt + 1], m_all[:, qt:qt + 1], ALU.mult))
            vgap()
            svv = sch.op("vector", lambda e, q0=q0: e.tensor_tensor(
                qstat[:, q0 + 4:q0 + 5], qstat[:, q0 + 2:q0 + 3],
                qstat[:, q0 + 3:q0 + 4], ALU.subtract), inc=SV)
            ssl = sch.op("scalar", lambda e, q0=q0: e.activation(
                qstat[:, q0 + 5:q0 + 6], qstat[:, q0 + 4:q0 + 5], AF.Ln,
                bias=cEPS[:, :], scale=1.0), waits=[(SV, svv)], inc=SS)
            ssr = sch.op("scalar", lambda e, qt=qt, q0=q0: e.activation(
                rstd_all[:, qt:qt + 1], qstat[:, q0 + 5:q0 + 6], AF.Exp,
                bias=0.0, scale=cN05[:, :]), waits=[(SS, ssl)], inc=SS)
            sch.op("vector", lambda e, qt=qt, q0=q0: e.reduce_max(
                qstat[:, q0 + 6:q0 + 7], x1t[:, (qt % 2) * 1024:(qt % 2 + 1) * 1024], axis=AX.X))
            sch.op("vector", lambda e, qt=qt, q0=q0: e.tensor_reduce(
                qstat[:, q0 + 7:q0 + 8], x1t[:, (qt % 2) * 1024:(qt % 2 + 1) * 1024],
                axis=AX.X, op=ALU.min))
            sch.op("vector", lambda e, qt=qt, q0=q0: e.tensor_tensor(
                qstat[:, q0 + 6:q0 + 7], qstat[:, q0 + 6:q0 + 7], m_all[:, qt:qt + 1], ALU.subtract))
            sch.op("vector", lambda e, qt=qt, q0=q0: e.tensor_tensor(
                qstat[:, q0 + 7:q0 + 8], m_all[:, qt:qt + 1], qstat[:, q0 + 7:q0 + 8], ALU.subtract))
            vgap()
            sch.op("vector", lambda e, q0=q0: e.tensor_tensor(
                qstat[:, q0 + 6:q0 + 7], qstat[:, q0 + 6:q0 + 7], qstat[:, q0 + 7:q0 + 8], ALU.max))
            vgap()
            sch.op("vector", lambda e, qt=qt, q0=q0: e.tensor_tensor(
                qstat[:, q0 + 6:q0 + 7], qstat[:, q0 + 6:q0 + 7], rstd_all[:, qt:qt + 1], ALU.mult),
                waits=[(SS, ssr)])
            vgap()
            sch.op("vector", lambda e, q0=q0: e.tensor_tensor(
                absx[:, :], absx[:, :], qstat[:, q0 + 6:q0 + 7], ALU.max))

        NU = 32
        H = {}
        pieces, kvr_next = produce_pieces(G + 1) if G < 3 else ([], None)
        ppi = 0
        pend_ln2 = {}
        for i in range(NU + 7):
            if i < NU:
                pa = stageA(i)
                v1 = stageV1(i, pa)
                H[i] = {"pa": pa, "v1": v1}
            u = i - 1
            if 0 <= u < NU:
                ws = []
                if (u - 2) in H and "pb" in H[u - 2]:
                    ws.append(H[u - 2]["pb"])
                if (u - 2) in H and "v2" in H[u - 2]:
                    ws.append((SV, H[u - 2]["v2"]))
                for w_ in ws:
                    sch.wait("scalar", w_[0], w_[1])
                H[u]["s1"] = stageS1(u, H[u]["v1"])
            u = i - 2
            if 0 <= u < NU:
                if (u - 2) in H and "pb" in H[u - 2]:
                    sch.wait("vector", H[u - 2]["pb"][0], H[u - 2]["pb"][1])
                H[u]["v2"] = stageV2(u, H[u]["s1"])
                H[u]["pb"] = stageB(u, H[u]["v2"], H[u]["s1"])
                ws = []
                if (u - 2) in H and "pc" in H[u - 2]:
                    sch.wait("scalar", H[u - 2]["pc"][0], H[u - 2]["pc"][1])
                H[u]["s2"] = stageS2(u, H[u]["pb"])
            u = i - 3
            if 0 <= u < NU:
                H[u]["pc"] = stageC(u, H[u]["s2"])
                stageS3(u, H[u]["pc"])
                if u % 8 == 7:
                    finish_qt(4 * G + u // 8)
                    pend_ln2[i + 4] = 4 * G + u // 8
            if i in pend_ln2:
                ln2_stats(pend_ln2.pop(i))
            while ppi < len(pieces) and pieces[ppi][0] <= i:
                pieces[ppi][1]()
                ppi += 1
        assert ppi == len(pieces)
        sskv_group = SS.n
        cur_kvr = kvr_next

    stx = sch.op("tensor", lambda e: e.matmul(
        psl[0:1, 0:128], absx[:, :], identf[:, :], start=True, stop=True),
        waits=ps_w("psl") + [(SV, SV.n)], inc=ST)
    psprod["psl"] = (ST, stx)
    svr = sch.op("vector", lambda e: e.tensor_copy(ccrow[0:1, :], psl[0:1, 0:128]),
                 waits=[psprod["psl"]], inc=SV)
    pscons["psl"] = (SV, svr)

    if _DEBUG.get("stop") == "x1":
        sch.wait("sync", SX, SX.n)
        sch.wait("gpsimd", SG, SG.n)
        with nc.Block() as block:
            sch.play(block)
        esb.close()
        es0.close()
        return nc

    if _DEBUG.get("scales"):
        sch.op("vector", lambda e: e.tensor_copy(t11[0:1, 2:3], t11[0:1, 2:3]))
        sch.op("sync", lambda e: e.dma_start(out=dbg_sc[0:1, 0:1], in_=rxv[:, :]),
               waits=[(SV, SV.n)], inc=SX, inc_n=16)
        sch.op("sync", lambda e: e.dma_start(out=dbg_sc[0:1, 1:2], in_=sxv[:, :]), inc=SX, inc_n=16)
        sch.op("sync", lambda e: e.dma_start(out=dbg_sc[0:1, 2:3], in_=sigx[:, :]), inc=SX, inc_n=16)
        sch.op("sync", lambda e: e.dma_start(out=dbg_sc[0:1, 3:4], in_=sigd[:, :]), inc=SX, inc_n=16)

    with nc.Block() as block:
        sch.play(block)
    esb.close()

    esc = ExitStack()

    def sbC(name, shape, dtype):
        return esc.enter_context(nc.sbuf_tensor(name, list(shape), dtype))

    g2 = sbC("g2", [128, IC * 2048], BF16)
    qwdTa = sbC("qwdTa", [128, IC * 512], BF16)
    srow = sbC("srow", [1, E], F32)
    wdn = sbC("wdn", [128, II], F32)
    rmd = sbC("rmd", [128, 8], F32)
    cgb = sbC("cgb", [128, 4], F32)
    cub = sbC("cub", [128, 4], F32)
    rwb = sbC("rwb", [128, 8], F32)
    rmg = sbC("rmg", [128, 8], F32)
    aqb = sbC("aqb", [128, 4], F32)

    esct = ExitStack()

    def sbT(name, shape, dtype):
        return esct.enter_context(nc.sbuf_tensor(name, list(shape), dtype))

    xqf = sbT("xqf", [128, 8 * 2048], BF16)
    qwg = sbT("qwg", [128, 4 * 8 * 128], BF16)
    qwu = sbT("qwu", [128, 4 * 8 * 128], BF16)
    tmq = sbT("tmq", [128, 1024], BF16)
    silu2 = sbT("silu2", [128, 2 * 512], F32)
    wgn = sbT("wgn", [128, 4 * 1024], F32)

    tmq_lastT = [0]
    rw_lastS = {i: (SS, 0) for i in range(8)}
    qslot_ctr = [0]

    sch.wait("sync", SX, y_write_sx[15])
    sch.op("sync", lambda e: e.nop(), inc=SYW)
    sch.wait("gpsimd", SV, svr)
    sch.op("gpsimd", lambda e: e.dma_start(out=cc_in_x[:, :], in_=ccrow[0:1, :]),
           inc=SG, inc_n=16)
    sch.wait("gpsimd", SG, SG.n)
    scc1 = sch.op("gpsimd", lambda e: e.collective_compute(
        "AllReduce", ALU.max, replica_groups=[list(range(8))],
        ins=[cc_in_x.ap()], outs=[cc_out_x.ap()]), inc=SCC)
    sch.wait("gpsimd", SCC, scc1)
    sch.op("gpsimd", lambda e: e.dma_start(out=ccrow2[0:1, :], in_=cc_out_x[:, :]),
           inc=SG, inc_n=16)
    sgw_b1 = ckpt("gpsimd", SG, SGW)
    sch.op("scalar", load_table(SET_SILU))

    def quant_chunk(nat_col, ncols, rw_slot, rowmax_ap, ready_w, write_piece,
                    evac_w=(), scrb=None, piece_ss=None, pre=None):
        if pre is None:
            rw_slot = 6 + qslot_ctr[0] % 2
            qslot_ctr[0] += 1
            svm = sch.op("vector", lambda e: e.reduce_max(
                rowmax_ap, (wgn if ncols == 1024 else wdn)[:, nat_col:nat_col + ncols],
                axis=AX.X, apply_absolute_value=True),
                waits=list(ready_w) + [rw_lastS[rw_slot]], inc=SV)
            vgap()
            sch.op("vector", lambda e: e.reciprocal(tmpv[:, 0:1], rowmax_ap))
            vgap()
            svw = sch.op("vector", lambda e, rw_slot=rw_slot: e.tensor_scalar(
                rwb[:, rw_slot:rw_slot + 1], tmpv[:, 0:1],
                127.0, None, ALU.mult), inc=SV)
        else:
            rw_slot, svw = pre
            svm = 0
        ss_last = 0
        svq = 0
        if scrb is None:
            scrb = scr
        svw = max(svw, scr_free[0])
        gf = 0
        for p0 in range(0, ncols, 1024):
            pn = min(1024, ncols - p0)
            wq = [(SV, svw)] if svq == 0 else [(SS, svq)]
            ssq = sch.op("scalar", lambda e, p0=p0, pn=pn, nat_col=nat_col, rw_slot=rw_slot, ncols=ncols, scrb=scrb: e.activation(
                scrb[:, 0:pn], (wgn if ncols == 1024 else wdn)[:, nat_col + p0:nat_col + p0 + pn],
                AF.Identity, bias=cMAG[:, :], scale=rwb[:, rw_slot:rw_slot + 1]),
                waits=wq, inc=SS)
            if piece_ss is not None:
                piece_ss[p0] = ssq
            svq = sch.op("scalar", lambda e, pn=pn, scrb=scrb: e.activation(
                tmq[:, 0:pn], scrb[:, 0:pn], AF.Identity,
                bias=cNMG[:, :], scale=1.0),
                waits=[(SS, ssq), (ST, tmq_lastT[0])], inc=SS)
            npieces = pn // 128
            for g0 in range(0, npieces, 4):
                gn = min(4, npieces - g0)
                ps, name = (psl, "psl") if gf % 2 == 0 else (pss, "pss")
                gf += 1
                wt = ps_w(name) + [(SS, svq)]
                for c in range(g0, g0 + gn):
                    stv = sch.op("tensor", lambda e, ps=ps, c=c, g0=g0: e.matmul(
                        ps[:, (c - g0) * 128:(c - g0 + 1) * 128],
                        tmq[:, c * 128:(c + 1) * 128], identb[:, :],
                        start=True, stop=True),
                        waits=wt if c == g0 else [],
                        inc=ST if c == g0 + gn - 1 else None)
                psprod[name] = (ST, stv)
                tmq_lastT[0] = stv
                ss_last = sch.op("vector",
                                 lambda e, ps=ps, p0=p0, g0=g0, gn=gn: write_piece(
                                     e, ps, p0 // 128 + g0, gn),
                                 waits=[psprod[name]] + list(evac_w), inc=SV)
                pscons[name] = (SV, ss_last)
        rw_lastS[rw_slot] = (SV, ss_last)
        return (SV, ss_last), svm

    wgn_slot_reads = {}
    scr_free = [0]
    wd_scr = [None]
    wd_sched = {14: 0, 16: 1, 18: 2, 20: 3}
    swd_prev = [0]
    qwdTa_ss = (SS, 0)

    wd_piece_ss = {}

    def wd_issue(dc):
        ready = []
        for i, p0 in enumerate((0, 1024, 2048)):
            pn = min(1024, II - p0)
            w = [(SS, wd_piece_ss[p0])] if p0 in wd_piece_ss else [(SYW, 16)]
            sch.op("sync", lambda e, dc=dc, p0=p0, pn=pn: e.dma_start(
                out=wdn[:, p0:p0 + pn], in_=dwd[dc * 128:(dc + 1) * 128, p0:p0 + pn]),
                waits=w, inc=SDP[i], inc_n=16)
            ready.append((SDP[i], SDP[i].n))
        return ready

    def wd_quant(dc, dst_sb, dst_dcl, ready=None):
        if ready is None:
            ready = wd_issue(dc)

        def wp(e, ps, piece, gn, dst_sb=dst_sb, dst_dcl=dst_dcl):
            src = bass.AP(ps, 0, [[512, 128], [128, gn], [1, 128]])
            dst = bass.AP(dst_sb, piece * 512 + dst_dcl * 128,
                          [[IC * 512, 128], [512, gn], [1, 128]])
            return e.tensor_copy(dst, src)
        ss_last, svm = quant_chunk(0, II, 0, rmd[:, dc:dc + 1], ready, wp,
                                   scrb=wd_scr[0], piece_ss=wd_piece_ss)
        stv = sch.op("tensor", lambda e, dc=dc: e.matmul(
            pso[0:1, 0:128], rmd[:, dc:dc + 1], identf[:, :], start=True, stop=True),
            waits=ps_w("pso") + [(SV, svm)], inc=ST)
        psprod["pso"] = (ST, stv)
        svsr = sch.op("vector", lambda e, dc=dc: e.tensor_copy(
            srow[0:1, dc * 128:(dc + 1) * 128], pso[0:1, 0:128]),
            waits=[psprod["pso"]], inc=SV)
        pscons["pso"] = (SV, svsr)
        swd_prev[0] = ss_last
        return ss_last

    swp_val = {}
    pair_pre = {}

    def issue_pair(oc):
        gslot = (2 * oc) % 4
        uslot = (2 * oc + 1) % 4
        wg_w = [wgn_slot_reads[gslot]] if gslot in wgn_slot_reads else [(SYW, 16)]
        sch.op("sync", lambda e, oc=oc, gslot=gslot: e.dma_start(
            out=wgn[:, gslot * 1024:(gslot + 1) * 1024],
            in_=dwg[oc * 128:(oc + 1) * 128, :]), waits=wg_w, inc=SWP[gslot], inc_n=16)
        swp_val[gslot] = SWP[gslot].n
        wu_w = [wgn_slot_reads[uslot]] if uslot in wgn_slot_reads else [(SYW, 16)]
        sch.op("sync", lambda e, oc=oc, uslot=uslot: e.dma_start(
            out=wgn[:, uslot * 1024:(uslot + 1) * 1024],
            in_=dwu[oc * 128:(oc + 1) * 128, :]), waits=wu_w, inc=SWP[uslot], inc_n=16)
        swp_val[uslot] = SWP[uslot].n
        par = oc % 4
        pre = {}
        for which, slot_ld, rs in (("g", gslot, par * 2), ("u", uslot, par * 2 + 1)):
            svm = sch.op("vector", lambda e, slot_ld=slot_ld, rs=rs: e.reduce_max(
                rmg[:, rs:rs + 1], wgn[:, slot_ld * 1024:(slot_ld + 1) * 1024],
                axis=AX.X, apply_absolute_value=True),
                waits=[(SWP[slot_ld], swp_val[slot_ld]), rw_lastS[rs]], inc=SV)
            vgap()
            sch.op("vector", lambda e, rs=rs: e.reciprocal(
                tmpv[:, 0:1], rmg[:, rs:rs + 1]))
            vgap()
            svw = sch.op("vector", lambda e, rs=rs: e.tensor_scalar(
                rwb[:, rs:rs + 1], tmpv[:, 0:1], 127.0, None, ALU.mult), inc=SV)
            pre[which] = (rs, svw)
        pair_pre[oc] = pre

    sv_absg = 0
    prev_svg = {0: 0, 1: 0}
    qw_lastT = {0: 0, 1: 0, 2: 0, 3: 0}

    def quant_gate(oc):
        gslot = (2 * oc) % 4
        par = oc % 4
        evw = [(ST, qw_lastT[par])] if qw_lastT[par] else []
        pre = pair_pre.pop(oc)

        def wpg(e, ps, piece, gn, par=par):
            return e.tensor_copy(
                qwg[:, par * 1024 + piece * 128:par * 1024 + (piece + gn) * 128],
                ps[:, 0:gn * 128])
        ssg, _ = quant_chunk(gslot * 1024, 1024, 0, None, [], wpg,
                             evac_w=evw, pre=pre["g"])
        wgn_slot_reads[gslot] = ssg
        return dict(ssg=ssg, par=par, pre=pre, evw=evw, oc=oc)

    def quant_up(q):
        oc = q["oc"]
        par = q["par"]
        uslot = (2 * oc + 1) % 4

        def wpu(e, ps, piece, gn, par=par):
            return e.tensor_copy(
                qwu[:, par * 1024 + piece * 128:par * 1024 + (piece + gn) * 128],
                ps[:, 0:gn * 128])
        ssu, _ = quant_chunk(uslot * 1024, 1024, 1, None, [], wpu,
                             evac_w=q["evw"], pre=q["pre"]["u"])
        wgn_slot_reads[uslot] = ssu
        q["ssu"] = ssu
        return q

    def quant_pair(oc):
        return quant_up(quant_gate(oc))

    def pair_scales(q):
        par = q["par"]
        q["svcg"] = sch.op("vector", lambda e, par=par: e.scalar_tensor_tensor(
            cgb[:, par:par + 1], rmg[:, par * 2:par * 2 + 1], 1.0 / 127.0, sxb[:, :],
            ALU.mult, ALU.mult), waits=[(SV, sv_rxb)], inc=SV)
        q["svcu"] = sch.op("vector", lambda e, par=par: e.scalar_tensor_tensor(
            cub[:, par:par + 1], rmg[:, par * 2 + 1:par * 2 + 2], 1.0 / 127.0, sxb[:, :],
            ALU.mult, ALU.mult), inc=SV)
        vgap()

    def gate_up_unit(oc, Gt, q):
        par = q["par"]
        u = (oc * 4 + Gt) % 2
        gps, gnm = (pq, "pq") if u == 0 else (pst, "pst")
        ups, unm = (pk, "pk") if u == 0 else (psp, "psp")
        wg2 = ps_w(gnm) + [q["ssg"], (SS, ss_xq_done)]
        for fcx in range(FC):
            stv = sch.op("tensor", lambda e, gps=gps, fcx=fcx, par=par, Gt=Gt: e.matmul(
                gps[:, :] if gps is not psp else gps[:, 0:512],
                qwg[:, par * 1024 + fcx * 128:par * 1024 + (fcx + 1) * 128],
                xqf[:, fcx * 2048 + Gt * 512:fcx * 2048 + (Gt + 1) * 512],
                start=(fcx == 0), stop=(fcx == FC - 1)),
                waits=wg2 if fcx == 0 else [], inc=ST if fcx == FC - 1 else None)
        psprod[gnm] = (ST, stv)
        wu2 = ps_w(unm) + [q["ssu"]]
        for fcx in range(FC):
            stv = sch.op("tensor", lambda e, ups=ups, fcx=fcx, par=par, Gt=Gt: e.matmul(
                ups[:, :] if ups is not psp else ups[:, 0:512],
                qwu[:, par * 1024 + fcx * 128:par * 1024 + (fcx + 1) * 128],
                xqf[:, fcx * 2048 + Gt * 512:fcx * 2048 + (Gt + 1) * 512],
                start=(fcx == 0), stop=(fcx == FC - 1)),
                waits=wu2 if fcx == 0 else [], inc=ST if fcx == FC - 1 else None)
        psprod[unm] = (ST, stv)
        qw_lastT[par] = stv
        sil = sch.op("scalar", lambda e, gps=gps, u=u, par=par: e.activation(
            silu2[:, u * 512:(u + 1) * 512],
            gps[:, :] if gps is not psp else gps[:, 0:512], AF.Silu,
            bias=0.0, scale=cgb[:, par:par + 1]),
            waits=[psprod[gnm], (SV, q["svcg"]), (SV, prev_svg[u])], inc=SS)
        pscons[gnm] = (SS, sil)
        svg = sch.op("vector", lambda e, ups=ups, u=u, par=par, oc=oc, Gt=Gt: e.scalar_tensor_tensor(
            g2[:, oc * 2048 + Gt * 512:oc * 2048 + (Gt + 1) * 512],
            ups[:, :] if ups is not psp else ups[:, 0:512],
            cub[:, par:par + 1], silu2[:, u * 512:(u + 1) * 512],
            ALU.mult, ALU.mult),
            waits=[psprod[unm], (SS, sil), (SV, q["svcu"])], inc=SV)
        pscons[unm] = (SV, svg)
        prev_svg[u] = svg
        vgap()
        sch.op("vector", lambda e, oc=oc, Gt=Gt: e.reduce_max(
            tmpv[:, 1:2], g2[:, oc * 2048 + Gt * 512:oc * 2048 + (Gt + 1) * 512],
            axis=AX.X, apply_absolute_value=True))
        vgap()
        return sch.op("vector", lambda e: e.tensor_tensor(
            absg[:, :], absg[:, :], tmpv[:, 1:2], ALU.max), inc=SV)

    QP = {}
    nextq = [0]
    nextl = [1]

    def advance_quant():
        if nextq[0] < IC:
            q = quant_pair(nextq[0])
            pair_scales(q)
            QP[nextq[0]] = q
            nextq[0] += 1
        if nextl[0] < IC:
            issue_pair(nextl[0])
            nextl[0] += 1

    issue_pair(0)
    qwdTa_ss = wd_quant(0, qwdTa, 0)
    issue_pair(1)
    QP[0] = quant_pair(0)
    qwdTa_ss = wd_quant(1, qwdTa, 1)
    issue_pair(2)
    QP[1] = quant_pair(1)
    qwdTa_ss = wd_quant(2, qwdTa, 2)
    QP[2] = quant_pair(2)
    qwdTa_ss = wd_quant(3, qwdTa, 3)
    nextq[0] = 3
    nextl[0] = 4

    sch.op("vector", lambda e: e.reduce_max(t11[0:1, 2:3], ccrow2[0:1, :], axis=AX.X),
           waits=[(SGW, sgw_b1), (SV, sv_sig)])
    vgap()
    sch.op("vector", lambda e: e.tensor_tensor(t11[0:1, 2:3], t11[0:1, 2:3], sigx[:, :], ALU.mult))
    vgap()
    sch.op("vector", lambda e: e.reciprocal(t11[0:1, 3:4], t11[0:1, 2:3]))
    vgap()
    sch.op("vector", lambda e: e.tensor_scalar(rxv[:, :], t11[0:1, 3:4], 127.0, None, ALU.mult))
    svsc = sch.op("vector", lambda e: e.tensor_scalar(
        sxv[:, :], t11[0:1, 2:3], 1.0 / 127.0, None, ALU.mult), inc=SV)
    sch.op("tensor", lambda e: e.matmul(
        pso[:, 0:1], onesf[:, :], rxv[0:1, 0:1], start=True, stop=True),
        waits=ps_w("pso") + [(SV, svsc)])
    stb2 = sch.op("tensor", lambda e: e.matmul(
        pso[:, 1:2], onesf[:, :], sxv[0:1, 0:1], start=True, stop=True), inc=ST)
    psprod["pso"] = (ST, stb2)
    sch.op("vector", lambda e: e.tensor_copy(rxb[:, :], pso[:, 0:1]), waits=[psprod["pso"]])
    sv_rxb = sch.op("vector", lambda e: e.tensor_copy(sxb[:, :], pso[:, 1:2]), inc=SV)
    pscons["pso"] = (SV, sv_rxb)
    pair_scales(QP[0])
    pair_scales(QP[1])
    pair_scales(QP[2])
    issue_pair(3)

    ss_prequant = SS.n
    xq_read_sv = {}
    xqbuf = [scr, silu2]
    for qt in range(NQT):
        par = qt % 2
        xb = xqbuf[par]
        w = [(SYW, 16)]
        if par == 0:
            w.append((SS, ss_prequant))
        if (qt - 2) in xq_read_sv:
            w.append((SV, xq_read_sv[qt - 2]))
        sch.op("sync", lambda e, qt=qt, xb=xb: e.dma_start(
            out=xb[:, :], in_=dy[qt * 128:(qt + 1) * 128, :]),
            waits=w, inc=SX, inc_n=16)
        sx_r = SX.n
        a0 = par * 2
        sch.op("vector", lambda e, qt=qt, a0=a0: e.tensor_tensor(
            aqb[:, a0:a0 + 1], rstd_all[:, qt:qt + 1], rxb[:, :], ALU.mult))
        vgap()
        sch.op("vector", lambda e, qt=qt, a0=a0: e.tensor_tensor(
            aqb[:, a0 + 1:a0 + 2], m_all[:, qt:qt + 1], aqb[:, a0:a0 + 1], ALU.mult))
        vgap()
        svab = sch.op("vector", lambda e, a0=a0: e.tensor_scalar(
            aqb[:, a0 + 1:a0 + 2], aqb[:, a0 + 1:a0 + 2], -1.0, CMAGIC,
            ALU.mult, ALU.add), inc=SV)
        ssq = sch.op("scalar", lambda e, a0=a0, xb=xb: e.activation(
            xb[:, :], xb[:, :],
            AF.Identity, bias=aqb[:, a0 + 1:a0 + 2], scale=aqb[:, a0:a0 + 1]),
            waits=[(SX, sx_r), (SV, svab)], inc=SS)
        sch.op("vector", lambda e, xb=xb: e.tensor_scalar(
            xb[:, :], xb[:, :],
            -CMAGIC, 127.0, ALU.add, ALU.min),
            waits=[(SS, ssq), (ST, tmq_lastT[0])])
        vgap()
        svq = sch.op("vector", lambda e, xb=xb: e.tensor_scalar(
            tmq[:, :], xb[:, :], -128.0, None, ALU.max), inc=SV)
        xq_read_sv[qt] = svq
        for half in range(2):
            ps, name = (psl, "psl") if half == 0 else (pss, "pss")
            wt = ps_w(name) + [(SV, svq)]
            for c4 in range(4):
                c = half * 4 + c4
                stv = sch.op("tensor", lambda e, ps=ps, c4=c4, c=c: e.matmul(
                    ps[:, c4 * 128:(c4 + 1) * 128],
                    tmq[:, c * 128:(c + 1) * 128], identb[:, :],
                    start=True, stop=True),
                    waits=wt if c4 == 0 else [], inc=ST if c4 == 3 else None)
            psprod[name] = (ST, stv)
            tmq_lastT[0] = stv

            def evq(e, ps=ps, qt=qt, half=half):
                src = bass.AP(ps, 0, [[512, 128], [128, 4], [1, 128]])
                dst = bass.AP(xqf, half * 4 * 2048 + qt * 128,
                              [[16384, 128], [2048, 4], [1, 128]])
                return e.copy(dst, src)
            ssv = sch.op("scalar", evq, waits=[psprod[name]], inc=SS)
            pscons[name] = (SS, ssv)
    ss_xq_done = SS.n
    scr_free[0] = xq_read_sv[14]

    if _DEBUG.get("stop") == "xq":
        sch.wait("gpsimd", SS, ss_xq_done)
        sch.op("gpsimd", lambda e: e.dma_start(out=dbg_xq[:, :], in_=xqf[:, :]), inc=SG, inc_n=16)
        sch.wait("gpsimd", SG, SG.n)
        sch.wait("sync", SX, SX.n)
        with nc.Block() as block:
            sch.play(block)
        esct.close()
        esc.close()
        es0.close()
        return nc


    for oc in range(IC):
        adv = nextq[0] <= min(oc + 3, IC - 1)
        if adv:
            k = nextq[0]
            qpart = quant_gate(k)
        q = QP.pop(oc)
        sv_absg = gate_up_unit(oc, 0, q)
        if adv:
            quant_up(qpart)
            pair_scales(qpart)
            QP[k] = qpart
            nextq[0] += 1
        sv_absg = gate_up_unit(oc, 1, q)
        if adv and nextl[0] < IC:
            issue_pair(nextl[0])
            nextl[0] += 1
        sv_absg = gate_up_unit(oc, 2, q)
        sv_absg = gate_up_unit(oc, 3, q)
    wd4_ready = wd_issue(4)

    if _DEBUG.get("stop") == "g":
        sch.wait("gpsimd", SV, sv_absg)
        sch.op("gpsimd", lambda e: e.dma_start(out=dbg_g[:, :], in_=g2[:, :]), inc=SG, inc_n=16)
        sch.wait("gpsimd", SG, SG.n)
        sch.wait("sync", SX, SX.n)
        with nc.Block() as block:
            sch.play(block)
        esct.close()
        esc.close()
        es0.close()
        return nc

    with nc.Block() as block:
        sch.play(block)
    esct.close()

    esd = ExitStack()
    qwdTb = esd.enter_context(nc.sbuf_tensor("qwdTb", [128, IC * 512], BF16))
    otile = esd.enter_context(nc.sbuf_tensor("otile", [128, 2 * 512], F32))
    tmq2 = esd.enter_context(nc.sbuf_tensor("tmq2", [128, 1024], BF16))
    sdb = esd.enter_context(nc.sbuf_tensor("sdb", [128, E], F32))
    scrD = esd.enter_context(nc.sbuf_tensor("scrD", [128, 1024], F32))
    tmq = tmq2
    tmq_lastT[0] = 0
    wd_scr[0] = scrD

    stx = sch.op("tensor", lambda e: e.matmul(
        psl[0:1, 0:128], absg[:, :], identf[:, :], start=True, stop=True),
        waits=ps_w("psl") + [(SV, sv_absg)], inc=ST)
    psprod["psl"] = (ST, stx)
    svr = sch.op("vector", lambda e: e.tensor_copy(ccrow[0:1, :], psl[0:1, 0:128]),
                 waits=[psprod["psl"]], inc=SV)
    pscons["psl"] = (SV, svr)
    sch.wait("gpsimd", SV, svr)
    sch.op("gpsimd", lambda e: e.dma_start(out=cc_in_g[:, :], in_=ccrow[0:1, :]),
           inc=SG, inc_n=16)
    sch.wait("gpsimd", SG, SG.n)
    scc2 = sch.op("gpsimd", lambda e: e.collective_compute(
        "AllReduce", ALU.max, replica_groups=[list(range(8))],
        ins=[cc_in_g.ap()], outs=[cc_out_g.ap()]), inc=SCC)
    sch.wait("gpsimd", SCC, scc2)
    sch.op("gpsimd", lambda e: e.dma_start(out=ccrow2[0:1, :], in_=cc_out_g[:, :]),
           inc=SG, inc_n=16)
    sgw_b2 = ckpt("gpsimd", SG, SGW)
    qwdTb_ss = wd_quant(4, qwdTb, 0, ready=wd4_ready)
    qwdTb_ss = wd_quant(5, qwdTb, 1)
    sch.op("vector", lambda e: e.reduce_max(t11[0:1, 2:3], ccrow2[0:1, :], axis=AX.X),
           waits=[(SGW, sgw_b2)])
    vgap()
    sch.op("vector", lambda e: e.tensor_tensor(t11[0:1, 2:3], t11[0:1, 2:3], sigd[:, :], ALU.mult))
    vgap()
    sch.op("vector", lambda e: e.reciprocal(t11[0:1, 3:4], t11[0:1, 2:3]))
    vgap()
    sch.op("vector", lambda e: e.tensor_scalar(rhv[:, :], t11[0:1, 3:4], 127.0, None, ALU.mult))
    svsc2 = sch.op("vector", lambda e: e.tensor_scalar(
        shv[:, :], t11[0:1, 2:3], 1.0 / (127.0 * 127.0), None, ALU.mult), inc=SV)
    sch.op("tensor", lambda e: e.matmul(
        pso[:, 0:1], onesf[:, :], rhv[0:1, 0:1], start=True, stop=True),
        waits=ps_w("pso") + [(SV, svsc2)])
    stb = sch.op("tensor", lambda e: e.matmul(
        pso[:, 2:3], onesf[:, :], shv[0:1, 0:1], start=True, stop=True), inc=ST)
    psprod["pso"] = (ST, stb)
    sch.op("vector", lambda e: e.tensor_copy(rhb[:, :], pso[:, 0:1]),
           waits=[psprod["pso"]])
    svb2 = sch.op("vector", lambda e: e.tensor_copy(shb[:, :], pso[:, 2:3]), inc=SV)
    vgap()
    pscons["pso"] = (SV, svb2)

    if _DEBUG.get("scales"):
        sch.op("sync", lambda e: e.dma_start(out=dbg_sc[0:1, 4:5], in_=rhv[:, :]),
               waits=[(SV, SV.n)], inc=SX, inc_n=16)
        sch.op("sync", lambda e: e.dma_start(out=dbg_sc[0:1, 5:6], in_=shv[:, :]), inc=SX, inc_n=16)
        sch.op("sync", lambda e: e.dma_start(out=dbg_sc[0:1, 6:7], in_=absx[0:1, 0:1]), inc=SX, inc_n=16)
        sch.op("sync", lambda e: e.dma_start(out=dbg_sc[0:1, 7:8], in_=absg[0:1, 0:1]), inc=SX, inc_n=16)

    stv = sch.op("tensor", lambda e: e.matmul(
        pst[:, :], onesf[:, :], srow[0:1, 0:512], start=True, stop=True),
        waits=ps_w("pst") + [(SV, svb2)], inc=ST)
    psprod["pst"] = (ST, stv)
    sv_sdb = sch.op("vector", lambda e: e.tensor_scalar(
        sdb[:, 0:512], pst[:, :], shb[:, :], None, ALU.mult),
        waits=[psprod["pst"]], inc=SV)
    pscons["pst"] = (SV, sv_sdb)

    acc_n = [0]
    dsec_n = [0]
    sv_sdb1 = sv_sdb
    h_sv = {}
    h_done = {}
    hu = 0

    def d_section(half, qt):
        nonlocal qwdTb_ss, sv_sdb1
        src_w = qwdTa if half == 0 else qwdTb
        need_ss = qwdTa_ss if half == 0 else qwdTb_ss
        u = dsec_n[0] % 2
        dsec_n[0] += 1
        pname = "pspD0" if u == 0 else "pspD1"
        ppslice = psp[:, u * 512:(u + 1) * 512]
        row = qt // 4
        fine = (half == 0 and qt % 4 == 0)
        hw0 = (SS, h_fine[(row, 0)]) if fine else (SS, h_done[row])
        w = ps_w(pname) + [need_ss, hw0]
        if pname not in pscons and "psp" in pscons:
            w = w + [pscons["psp"]]
        for ic in range(IC):
            mmw = w if ic == 0 else (
                [(SS, h_fine[(row, ic)])] if fine else [])
            stv = sch.op("tensor",
                lambda e, ppslice=ppslice, ic=ic, qt=qt, src_w=src_w: e.matmul(
                    ppslice, g2[:, ic * 2048 + qt * 128:ic * 2048 + (qt + 1) * 128],
                    src_w[:, ic * 512:(ic + 1) * 512],
                    start=(ic == 0), stop=(ic == IC - 1)),
                waits=mmw, inc=ST if ic == IC - 1 else None)
        psprod[pname] = (ST, stv)
        wvo = [psprod[pname], (SV, sv_sdb1 if half else sv_sdb)]
        j = acc_n[0] + 1
        if j >= 3:
            wvo.append((SGA, j - 2))
        svo = sch.op("vector", lambda e, ppslice=ppslice, u=u, half=half: e.tensor_tensor(
            otile[:, u * 512:(u + 1) * 512], ppslice,
            sdb[:, half * 512:(half + 1) * 512], ALU.mult),
            waits=wvo, inc=SV)
        pscons[pname] = (SV, svo)
        if j >= 2:
            ckpt("gpsimd", SG, SGA)
        sch.op("gpsimd", lambda e, qt=qt, u=u, half=half: e.dma_start(
            out=dy[qt * 128:(qt + 1) * 128, half * 512:(half + 1) * 512],
            in_=otile[:, u * 512:(u + 1) * 512], accum_op=ALU.add),
            waits=[(SV, svo), (SYW, 16)], inc=SG, inc_n=16)
        acc_n[0] = j
        if half == 0 and qt in (0, 2):
            dc = 6 + qt // 2
            qwdTb_ss = wd_quant(dc, qwdTb, dc - 4)
            if dc == 7:
                stv = sch.op("tensor", lambda e: e.matmul(
                    pst[:, :], onesf[:, :], srow[0:1, 512:1024],
                    start=True, stop=True),
                    waits=ps_w("pst") + [(SV, SV.n)], inc=ST)
                psprod["pst"] = (ST, stv)
                sv_sdb1 = sch.op("vector", lambda e: e.tensor_scalar(
                    sdb[:, 512:1024], pst[:, :], shb[:, :], None, ALU.mult),
                    waits=[psprod["pst"]], inc=SV)
                pscons["pst"] = (SV, sv_sdb1)

    pend = None

    h_fine = {}

    def h_sub(p):
        oc, Gt, hp2, svc = p
        sch.op("scalar", lambda e, oc=oc, Gt=Gt, hp2=hp2: e.activation(
            g2[:, oc * 2048 + Gt * 512:oc * 2048 + (Gt + 1) * 512],
            scr[:, hp2:hp2 + 512], AF.Identity, bias=cNMG[:, :], scale=1.0),
            waits=[(SV, svc)], inc=SS)
        h_fine[(Gt, oc)] = SS.n

    def h_unit(Gt, oc):
        nonlocal hu, pend
        hp2 = (hu % 2) * 512
        ssq = sch.op("scalar", lambda e, oc=oc, Gt=Gt, hp2=hp2: e.activation(
            scr[:, hp2:hp2 + 512], g2[:, oc * 2048 + Gt * 512:oc * 2048 + (Gt + 1) * 512],
            AF.Identity, bias=cMAG[:, :], scale=rhb[:, 0:1]),
            waits=[(SV, svb2)], inc=SS)
        svc = sch.op("vector", lambda e, hp2=hp2: e.tensor_scalar(
            scr[:, hp2:hp2 + 512], scr[:, hp2:hp2 + 512],
            CMAGIC + 127.0, CMAGIC - 128.0, ALU.min, ALU.max),
            waits=[(SS, ssq)], inc=SV)
        if pend is not None:
            h_sub(pend)
        pend = (oc, Gt, hp2, svc)
        hu += 1

    def h_flush(Gt):
        nonlocal pend
        h_sub(pend)
        pend = None
        h_done[Gt] = SS.n

    for oc in range(IC):
        h_unit(0, oc)
    h_flush(0)
    for Gt in range(4):
        nxt = Gt + 1
        k = 0
        for j, qt in enumerate(range(Gt * 4, Gt * 4 + 4)):
            d_section(0, qt)
            if Gt > 0:
                d_section(1, (Gt - 1) * 4 + j)
            if nxt < 4:
                target = IC if j == 3 else (j + 1) * 6
                while k < min(target, IC):
                    h_unit(nxt, k)
                    k += 1
        if nxt < 4:
            h_flush(nxt)
    for qt in range(12, 16):
        d_section(1, qt)

    sch.wait("gpsimd", SG, SG.n)
    sch.wait("sync", SX, SX.n)
    with nc.Block() as block:
        sch.play(block)

    esd.close()
    esc.close()
    es0.close()
    return nc


_CACHE = {}


def kernel(L0_attn_x, L0_attn_w_in, L0_attn_b_in, L0_attn_w_out, L0_attn_b_out,
           L0_ffn_clip_x, L0_ffn_clip_down, L0_ffn_w_gate, L0_ffn_w_up,
           L0_ffn_w_down):
    x = np.ascontiguousarray(np.asarray(L0_attn_x, dtype=np.float32))
    w_in = np.asarray(L0_attn_w_in, dtype=np.float32)
    b_in = np.asarray(L0_attn_b_in, dtype=np.float32)
    w_out = np.asarray(L0_attn_w_out, dtype=np.float32)
    b_out = np.asarray(L0_attn_b_out, dtype=np.float32)
    w_gate = np.ascontiguousarray(np.asarray(L0_ffn_w_gate, dtype=np.float32))
    w_up = np.ascontiguousarray(np.asarray(L0_ffn_w_up, dtype=np.float32))
    w_down = np.ascontiguousarray(np.asarray(L0_ffn_w_down, dtype=np.float32))
    clip_x = np.ascontiguousarray(np.asarray(L0_ffn_clip_x, dtype=np.float32).reshape(1, 1))
    clip_d = np.ascontiguousarray(np.asarray(L0_ffn_clip_down, dtype=np.float32).reshape(1, 1))

    w_inT = np.ascontiguousarray(w_in.T)
    w_outT = np.ascontiguousarray(w_out.T)
    b_pm = np.ascontiguousarray(b_in.reshape(24, 128).T)
    b_v_row = np.ascontiguousarray(b_in[2048:3072].reshape(1, 1024))
    b_out_row = np.ascontiguousarray(b_out.reshape(1, 1024))

    r = np.arange(128)[:, None]
    c = np.arange(256)[None, :]
    band_ok = (c - r >= 0) & (c - r <= 128)
    mask_band = np.where(band_ok, 0.0, NEG).astype(ml_dtypes.bfloat16)
    mask_first_h0 = np.where(band_ok & (c >= 128), 0.0, NEG).astype(ml_dtypes.bfloat16)
    ident_bf = np.eye(128, dtype=np.float32).astype(ml_dtypes.bfloat16)
    ident_f32 = np.eye(128, dtype=np.float32)
    ones_bf = np.ones((1, 128), np.float32).astype(ml_dtypes.bfloat16)
    ones_f32 = np.ones((1, 128), np.float32)
    zero_halo = np.zeros((128, 1024), np.float32)

    if "nc" not in _CACHE:
        _CACHE["nc"] = build_nc()
    nc = _CACHE["nc"]

    in_maps = []
    for core in range(8):
        b, h = core // 2, core % 2
        xm = np.ascontiguousarray(x[b, h * 2048:(h + 1) * 2048])
        xhl = zero_halo if h == 0 else np.ascontiguousarray(x[b, h * 2048 - 128:h * 2048])
        in_maps.append({
            "x_main": xm, "x_halo": xhl,
            "mask_first": mask_first_h0 if h == 0 else mask_band,
            "mask_band": mask_band,
            "w_inT": w_inT, "w_outT": w_outT, "b_pm": b_pm,
            "b_v_row": b_v_row, "b_out_row": b_out_row,
            "w_gate": w_gate, "w_up": w_up, "w_down": w_down,
            "clip_x": clip_x, "clip_down": clip_d,
            "ident_bf": ident_bf, "ident_f32": ident_f32,
            "ones_bf": ones_bf, "ones_f32": ones_f32,
        })

    res = run_bass_kernel_spmd(nc, in_maps, core_ids=list(range(8)))
    out = np.empty((4, 4096, 1024), dtype=np.float32)
    for core in range(8):
        b, h = core // 2, core % 2
        out[b, h * 2048:(h + 1) * 2048] = res.results[core]["y"]
    return out


# version 23
# speedup vs baseline: 1.0161x; 1.0161x over previous
import sys
import numpy as np

try:
    import concourse.bass as bass
except Exception:
    sys.path.insert(0, "/opt/trn_rl_repo")
    import concourse.bass as bass
import ml_dtypes
import concourse.mybir as mybir
from concourse.bass_utils import run_bass_kernel_spmd

F32 = mybir.dt.float32
BF16 = mybir.dt.bfloat16
AF = mybir.ActivationFunctionType
ALU = mybir.AluOpType
AX = mybir.AxisListType

CMAGIC = 12582912.0
NEG = -1.0e9
EPS = 1.0e-5
T = 2048
NQT = 16
E = 1024
I3 = 3072
II = 2816
IC = 22
FC = 8
ENGS = ("sync", "gpsimd", "scalar", "vector", "tensor")


class Sem:
    def __init__(self, nc, name):
        self.h = nc.alloc_semaphore(name)
        self.n = 0


class Sched:

    def __init__(self):
        self.q = {k: [] for k in ENGS}

    def op(self, eng, fn, waits=(), inc=None, inc_n=1):
        w = tuple((s, v) for (s, v) in waits if v > 0)
        if inc is not None:
            inc.n += inc_n

        def emit(e, fn=fn, w=w, inc=inc, inc_n=inc_n):
            for s, v in w:
                e.wait_ge(s.h, v)
            r = fn(e)
            if inc is not None:
                r.then_inc(inc.h, inc_n)
        self.q[eng].append(emit)
        return inc.n if inc is not None else None

    def wait(self, eng, sem, val):
        if val > 0:
            self.q[eng].append(lambda e, s=sem, v=val: e.wait_ge(s.h, v))

    def play(self, block):
        for name in ENGS:
            ops = self.q[name]
            if ops:
                getattr(block, name)(lambda eng, ops=ops: [f(eng) for f in ops])
        self.q = {k: [] for k in ENGS}


_DEBUG = {}


def build_nc():
    nc = bass.Bass("TRN2", target_bir_lowering=False, detect_race_conditions=False)

    dx = nc.dram_tensor("x_main", [T, E], F32, kind="ExternalInput")
    dxh = nc.dram_tensor("x_halo", [128, E], F32, kind="ExternalInput")
    dmf = nc.dram_tensor("mask_first", [128, 256], BF16, kind="ExternalInput")
    dmb = nc.dram_tensor("mask_band", [128, 256], BF16, kind="ExternalInput")
    dwin = nc.dram_tensor("w_inT", [E, I3], F32, kind="ExternalInput")
    dwout = nc.dram_tensor("w_outT", [E, E], F32, kind="ExternalInput")
    dbpm = nc.dram_tensor("b_pm", [128, 24], F32, kind="ExternalInput")
    dbv = nc.dram_tensor("b_v_row", [1, E], F32, kind="ExternalInput")
    dbo = nc.dram_tensor("b_out_row", [1, E], F32, kind="ExternalInput")
    dwg = nc.dram_tensor("w_gate", [II, E], F32, kind="ExternalInput")
    dwu = nc.dram_tensor("w_up", [II, E], F32, kind="ExternalInput")
    dwd = nc.dram_tensor("w_down", [E, II], F32, kind="ExternalInput")
    dcx = nc.dram_tensor("clip_x", [1, 1], F32, kind="ExternalInput")
    dcd = nc.dram_tensor("clip_down", [1, 1], F32, kind="ExternalInput")
    dib = nc.dram_tensor("ident_bf", [128, 128], BF16, kind="ExternalInput")
    dif = nc.dram_tensor("ident_f32", [128, 128], F32, kind="ExternalInput")
    donb = nc.dram_tensor("ones_bf", [1, 128], BF16, kind="ExternalInput")
    donf = nc.dram_tensor("ones_f32", [1, 128], F32, kind="ExternalInput")
    dy = nc.dram_tensor("y", [T, E], F32, kind="ExternalOutput")
    dbg_sc = nc.dram_tensor("dbg_sc", [1, 8], F32, kind="ExternalOutput") if _DEBUG.get("scales") else None
    dbg_xq = nc.dram_tensor("dbg_xq", [128, 8 * 2048], F32, kind="ExternalOutput") if _DEBUG.get("stop") == "xq" else None
    dbg_g = nc.dram_tensor("dbg_g", [128, IC * 2048], F32, kind="ExternalOutput") if _DEBUG.get("stop") == "g" else None
    if _DEBUG.get("stop") == "kv0":
        dbg_x = nc.dram_tensor("dbg_x", [128, 4096], F32, kind="ExternalOutput")
        dbg_ln1f = nc.dram_tensor("dbg_ln1f", [128, 8192], F32, kind="ExternalOutput")
        dbg_qf = nc.dram_tensor("dbg_qf", [128, 4096], F32, kind="ExternalOutput")
        dbg_kf = nc.dram_tensor("dbg_kf", [128, 8192], F32, kind="ExternalOutput")
        dbg_vt = nc.dram_tensor("dbg_vt", [128, 8192], F32, kind="ExternalOutput")
        dbg_st = nc.dram_tensor("dbg_st", [128, 8], F32, kind="ExternalOutput")
    cc_in_x = nc.dram_tensor("cc_in_x", [1, 128], F32)
    cc_out_x = nc.dram_tensor("cc_out_x", [1, 128], F32)
    cc_in_g = nc.dram_tensor("cc_in_g", [1, 128], F32)
    cc_out_g = nc.dram_tensor("cc_out_g", [1, 128], F32)

    SX = Sem(nc, "SX")
    SG = Sem(nc, "SG")
    SCC = Sem(nc, "SCC")
    ST = Sem(nc, "ST")
    SV = Sem(nc, "SV")
    SS = Sem(nc, "SS")
    SYP = Sem(nc, "SYP")
    SYW = Sem(nc, "SYW")
    SGW = Sem(nc, "SGW")
    SGD = Sem(nc, "SGD")
    SDP = [Sem(nc, "SDP%d" % i) for i in range(3)]
    SWP = [Sem(nc, "SWP%d" % i) for i in range(4)]
    SGA = Sem(nc, "SGA")

    from contextlib import ExitStack
    es0 = ExitStack()

    def sb0(name, shape, dtype):
        return es0.enter_context(nc.sbuf_tensor(name, list(shape), dtype))

    mfirst = sb0("mfirst", [128, 256], BF16)
    mband = sb0("mband", [128, 256], BF16)
    identb = sb0("identb", [128, 128], BF16)
    identf = sb0("identf", [128, 128], F32)
    onesb = sb0("onesb", [1, 128], BF16)
    onesf = sb0("onesf", [1, 128], F32)
    bsb = sb0("bsb", [128, 24], F32)
    bqs = sb0("bqs", [128, 8], F32)
    bvrow = sb0("bvrow", [1, E], BF16)
    borow = sb0("borow", [1, E], BF16)
    clipx = sb0("clipx", [1, 1], F32)
    clipd = sb0("clipd", [1, 1], F32)
    scr = sb0("scr", [128, 1024], F32)
    m_all = sb0("m_all", [128, NQT], F32)
    rstd_all = sb0("rstd_all", [128, NQT], F32)
    lnstat = sb0("lnstat", [128, 32], F32)
    qstat = sb0("qstat", [128, 16], F32)
    tmpv = sb0("tmpv", [128, 4], F32)
    absx = sb0("absx", [128, 1], F32)
    absg = sb0("absg", [128, 1], F32)
    sigx = sb0("sigx", [1, 1], F32)
    sigd = sb0("sigd", [1, 1], F32)
    t11 = sb0("t11", [1, 4], F32)
    rxv = sb0("rxv", [1, 1], F32)
    rhv = sb0("rhv", [1, 1], F32)
    sxv = sb0("sxv", [1, 1], F32)
    shv = sb0("shv", [1, 1], F32)
    rxb = sb0("rxb", [128, 1], F32)
    sxb = sb0("sxb", [128, 1], F32)
    rhb = sb0("rhb", [128, 1], F32)
    shb = sb0("shb", [128, 1], F32)
    ccrow = sb0("ccrow", [1, 128], F32)
    ccrow2 = sb0("ccrow2", [1, 128], F32)
    vjunk = sb0("vjunk", [128, 1], F32)
    cEPS = sb0("cEPS", [128, 1], F32)
    cMAG = sb0("cMAG", [128, 1], F32)
    cNMG = sb0("cNMG", [128, 1], F32)
    cN05 = sb0("cN05", [128, 1], F32)
    cN1 = sb0("cN1", [128, 1], F32)
    c18 = sb0("c18", [128, 1], F32)

    pq = es0.enter_context(nc.psum_tensor("pq", [128, 512], F32))
    pk = es0.enter_context(nc.psum_tensor("pk", [128, 512], F32))
    pss = es0.enter_context(nc.psum_tensor("pss", [128, 512], F32))
    pst = es0.enter_context(nc.psum_tensor("pst", [128, 512], F32))
    psl = es0.enter_context(nc.psum_tensor("psl", [128, 512], F32))
    pso = es0.enter_context(nc.psum_tensor("pso", [128, 128], F32))
    psp = es0.enter_context(nc.psum_tensor("psp", [128, 1024], F32))

    sch = Sched()
    psprod = {}
    pscons = {}

    def ps_w(name):
        return [pscons[name]] if name in pscons else []

    def vgap():
        sch.op("vector", lambda e: e.tensor_scalar(
            vjunk[:, :], cEPS[:, :], 1.0, None, ALU.mult))

    def ckpt(eng, dma_sem, prog_sem):
        sch.wait(eng, dma_sem, dma_sem.n)
        return sch.op(eng, lambda e: e.nop(), inc=prog_sem)

    esb = ExitStack()

    def sbB(name, shape, dtype):
        return esb.enter_context(nc.sbuf_tensor(name, list(shape), dtype))

    xring = sbB("xring", [128, 6 * 1024], F32)
    xh = sbB("xh", [128, 1024], F32)
    ln1b = sbB("ln1b", [128, 2 * 1024], BF16)
    ln1f = sbB("ln1f", [128, 2 * 4096], BF16)
    qf2 = sbB("qf2", [128, 2 * 4096], BF16)
    kf2 = sbB("kf2", [128, 8 * 1024], BF16)
    vtb = sbB("vtb", [128, 8 * 1024], BF16)
    winT = sbB("winT", [128, 8 * I3], BF16)
    woutT = sbB("woutT", [128, 8 * 1024], BF16)
    probs2 = sbB("probs2", [128, 2 * 512], BF16)
    probsT2 = sbB("probsT2", [128, 2 * 512], BF16)
    diag4 = sbB("diag4", [128, 4 * 128], BF16)
    nm2 = sbB("nm2", [128, 2], F32)
    sums4 = sbB("sums4", [128, 4], F32)
    rc4 = sbB("rc4", [128, 4], F32)
    ao2 = sbB("ao2", [128, 2 * 1024], BF16)
    x1t = sbB("x1t", [128, 2 * 1024], F32)
    x1s = sbB("x1s", [128, 4], F32)

    sch.op("vector", lambda e: e.memset(absx[:, :], 0.0))
    sch.op("vector", lambda e: e.memset(absg[:, :], 0.0))
    sch.op("vector", lambda e: e.memset(cEPS[:, :], EPS))
    sch.op("vector", lambda e: e.memset(cMAG[:, :], CMAGIC))
    sch.op("vector", lambda e: e.memset(cNMG[:, :], -CMAGIC))
    sch.op("vector", lambda e: e.memset(cN05[:, :], -0.5))
    sch.op("vector", lambda e: e.memset(cN1[:, :], -1.0))
    sv_consts = sch.op("vector", lambda e: e.memset(c18[:, :], 0.125), inc=SV)

    for dst, src in [(mfirst, dmf), (mband, dmb), (identb, dib), (identf, dif),
                     (onesb, donb), (onesf, donf), (bsb, dbpm), (clipx, dcx),
                     (clipd, dcd)]:
        sch.op("sync", lambda e, dst=dst, src=src: e.dma_start(
            out=dst[:, :], in_=src[:, :]), inc=SX, inc_n=16)
    sx_consts = SX.n

    sch.op("gpsimd", lambda e: e.dma_start(out=bvrow[:, :], in_=dbv[:, :]), inc=SG, inc_n=16)
    sch.op("gpsimd", lambda e: e.dma_start(out=borow[:, :], in_=dbo[:, :]), inc=SG, inc_n=16)
    for fc in range(FC):
        sch.op("gpsimd", lambda e, fc=fc: e.dma_start(
            out=winT[:, fc * I3:(fc + 1) * I3],
            in_=dwin[fc * 128:(fc + 1) * 128, :]), inc=SG, inc_n=16)
    sg_win = SG.n
    for fc in range(FC):
        sch.op("gpsimd", lambda e, fc=fc: e.dma_start(
            out=woutT[:, fc * 1024:(fc + 1) * 1024],
            in_=dwout[fc * 128:(fc + 1) * 128, :]), inc=SG, inc_n=16)
    sg_weights = SG.n

    from concourse.hw_specs import get_activation_tables
    tables = list(get_activation_tables(nc.m.arch).keys())
    SET_EXPLN = tables.index("natural_log_exp_and_others")
    SET_SILU = tables.index("silu_and_others")

    def load_table(idx):
        def f(e):
            return e.add_instruction(mybir.InstLoadActFuncSet(
                name=nc.get_next_instruction_name(), act_func_set_id=idx,
                ins=[], outs=[]))
        return f
    sch.op("scalar", load_table(SET_EXPLN))

    sv_bqs = sch.op("vector", lambda e: e.tensor_scalar(
        bqs[:, :], bsb[:, 0:8], 0.125, None, ALU.mult),
        waits=[(SYP, 1)], inc=SV)

    sch.op("scalar", lambda e: e.activation(
        t11[0:1, 0:1], clipx[:, :], AF.Exp, bias=0.0, scale=cN1[0:1, 0:1]),
        waits=[(SYP, 1), (SV, sv_consts)])
    ss_e2 = sch.op("scalar", lambda e: e.activation(
        t11[0:1, 1:2], clipd[:, :], AF.Exp, bias=0.0, scale=cN1[0:1, 0:1]), inc=SS)
    sch.op("vector", lambda e: e.tensor_scalar(
        t11[0:1, 0:2], t11[0:1, 0:2], 1.0, None, ALU.add), waits=[(SS, ss_e2)])
    vgap()
    sch.op("vector", lambda e: e.reciprocal(sigx[:, :], t11[0:1, 0:1]))
    sv_sig = sch.op("vector", lambda e: e.reciprocal(sigd[:, :], t11[0:1, 1:2]), inc=SV)

    x_load_sx = {}
    x_last_read = {}
    y_write_sx = {}
    kt_last_read = {}
    lnb_lastT = {}
    kv_write_ss = {}

    def xsl(mt, lo, n):
        return xring[:, (mt % 6) * 1024 + lo:(mt % 6) * 1024 + lo + n]

    def kcol(kt):
        return ((((kt - 1) // 4) % 2)) * 512 + ((kt - 1) % 4) * 128

    def vcol(kt):
        return ((((kt - 1) // 4) % 2)) * 4096 + ((kt - 1) % 4) * 1024

    def ln1_stats(mt, src, halo=False):
        p4 = 0 if halo else (mt % 4)
        c0 = p4 * 8
        w = [(SYP, 1 if halo else (mt // 4) + 1)]
        sch.op("vector", lambda e: e.tensor_reduce(
            lnstat[:, c0:c0 + 1], src(0, 1024), AX.X, ALU.add), waits=w)
        sch.op("vector", lambda e: e.scalar_tensor_tensor(
            scr[:, :], src(0, 1024), 1.0, src(0, 1024), ALU.mult, ALU.mult,
            accum_out=lnstat[:, c0 + 2:c0 + 3]))
        sch.op("vector", lambda e: e.tensor_scalar(
            lnstat[:, c0 + 1:c0 + 2], lnstat[:, c0:c0 + 1], 1.0 / 1024.0, None, ALU.mult))
        sch.op("vector", lambda e: e.tensor_scalar(
            lnstat[:, c0 + 3:c0 + 4], lnstat[:, c0 + 2:c0 + 3], 1.0 / 1024.0, None, ALU.mult))
        sch.op("vector", lambda e: e.tensor_tensor(
            lnstat[:, c0 + 4:c0 + 5], lnstat[:, c0 + 1:c0 + 2],
            lnstat[:, c0 + 1:c0 + 2], ALU.mult))
        vgap()
        svv = sch.op("vector", lambda e: e.tensor_tensor(
            lnstat[:, c0 + 5:c0 + 6], lnstat[:, c0 + 3:c0 + 4],
            lnstat[:, c0 + 4:c0 + 5], ALU.subtract), inc=SV)
        ssl = sch.op("scalar", lambda e: e.activation(
            lnstat[:, c0 + 6:c0 + 7], lnstat[:, c0 + 5:c0 + 6], AF.Ln,
            bias=cEPS[:, :], scale=1.0), waits=[(SV, svv)], inc=SS)
        ssr = sch.op("scalar", lambda e: e.activation(
            lnstat[:, c0 + 7:c0 + 8], lnstat[:, c0 + 6:c0 + 7], AF.Exp,
            bias=0.0, scale=cN05[:, :]), waits=[(SS, ssl)], inc=SS)
        return c0, ssr

    def ln1_norm(mt, src, c0, ssr, halo=False):
        par = 0 if halo else (mt % 2)
        w = [(SS, ssr)]
        if par in lnb_lastT:
            w.append((ST, lnb_lastT[par]))
        return par, sch.op("vector", lambda e, par=par: e.tensor_scalar(
            ln1b[:, par * 1024:(par + 1) * 1024], src(0, 1024),
            lnstat[:, c0 + 1:c0 + 2], lnstat[:, c0 + 7:c0 + 8],
            ALU.subtract, ALU.mult), waits=w, inc=SV)

    def ln1_transpose(par, dstbase, svn):
        w = ps_w("psp") + [(SV, svn), (SG, sg_win), (SYP, 1)]
        stv = None
        for half in range(2):
            for c4 in range(4):
                c = half * 4 + c4
                stv = sch.op("tensor", lambda e, half=half, c4=c4, c=c, par=par: e.matmul(
                    psp[:, half * 512 + c4 * 128:half * 512 + (c4 + 1) * 128],
                    ln1b[:, par * 1024 + c * 128:par * 1024 + (c + 1) * 128],
                    identb[:, :], start=True, stop=True),
                    waits=w if (half == 0 and c4 == 0) else [],
                    inc=ST if (half == 1 and c4 == 3) else None)
        psprod["psp"] = (ST, stv)
        lnb_lastT[par] = stv
        last_ss = None
        for half in range(2):
            def ev(e, dstbase=dstbase, half=half):
                src = bass.AP(psp, half * 512, [[1024, 128], [128, 4], [1, 128]])
                dst = bass.AP(ln1f, dstbase + half * 4 * 512,
                              [[8192, 128], [512, 4], [1, 128]])
                return e.copy(dst, src)
            last_ss = sch.op("scalar", ev,
                             waits=[psprod["psp"]] if half == 0 else [], inc=SS)
        pscons["psp"] = (SS, last_ss)
        return last_ss

    sch.op("sync", lambda e: e.dma_start(out=xh[:, :], in_=dxh[:, :]), inc=SX, inc_n=16)
    x_load_sx["halo"] = SX.n
    hsrc = lambda lo, n: xh[:, lo:lo + n]
    c0h, ssrh = ln1_stats(0, hsrc, halo=True)
    _, svh = ln1_norm(0, hsrc, c0h, ssrh, halo=True)
    ss_halo_T = ln1_transpose(0, 4096 + 3 * 128, svh)

    for oc in range(FC):
        w = ps_w("pk") + [(SS, ss_halo_T)]
        for fc in range(FC):
            stv = sch.op("tensor", lambda e, oc=oc, fc=fc: e.matmul(
                pk[:, 0:128],
                winT[:, fc * I3 + 1024 + oc * 128:fc * I3 + 1024 + (oc + 1) * 128],
                ln1f[:, 4096 + fc * 512 + 384:4096 + fc * 512 + 512],
                start=(fc == 0), stop=(fc == FC - 1)),
                waits=w if fc == 0 else [], inc=ST if fc == FC - 1 else None)
        psprod["pk"] = (ST, stv)
        ssv = sch.op("scalar", lambda e, oc=oc: e.activation(
            kf2[:, oc * 1024 + 512 + 384:oc * 1024 + 512 + 512], pk[:, 0:128],
            AF.Identity, bias=bsb[:, 8 + oc:9 + oc], scale=1.0),
            waits=[psprod["pk"]], inc=SS)
        pscons["pk"] = (SS, ssv)
    w = ps_w("psp")
    for n in range(2):
        sch.op("tensor", lambda e, n=n: e.matmul(
            psp[:, n * 512:(n + 1) * 512], onesb[:, :],
            bvrow[0:1, n * 512:(n + 1) * 512], start=True, stop=False),
            waits=w if n == 0 else [])
        for fc in range(FC):
            stv = sch.op("tensor", lambda e, n=n, fc=fc: e.matmul(
                psp[:, n * 512:(n + 1) * 512],
                ln1f[:, 4096 + fc * 512 + 384:4096 + fc * 512 + 512],
                winT[:, fc * I3 + 2048 + n * 512:fc * I3 + 2048 + (n + 1) * 512],
                start=False, stop=(fc == FC - 1)), inc=ST if fc == FC - 1 else None)
    psprod["psp"] = (ST, stv)
    ssv = sch.op("scalar", lambda e: e.copy(
        vtb[:, 4096 + 3 * 1024:4096 + 4 * 1024], psp[:, :]),
        waits=[psprod["psp"]], inc=SS)
    pscons["psp"] = (SS, ssv)

    def produce_pieces(G):
        gb = G % 2
        stats = {}
        lss = [None]
        kvr = {}

        def xload(mt, ck=False):
            def th(mt=mt, ck=ck):
                w = [x_last_read[mt - 6]] if (mt - 6) in x_last_read else []
                sch.op("sync", lambda e, mt=mt: e.dma_start(
                    out=xsl(mt, 0, 1024), in_=dx[mt * 128:(mt + 1) * 128, :]),
                    waits=w, inc=SX, inc_n=16)
                x_load_sx[mt] = SX.n
                if ck:
                    ckpt("sync", SX, SYP)
            return th

        def stat_p(tt):
            def th(tt=tt):
                mt = 4 * G + tt
                src = lambda lo, n, mt=mt: xsl(mt, lo, n)
                stats[mt] = ln1_stats(mt, src)
            return th

        def norm_p(tt):
            def th(tt=tt):
                mt = 4 * G + tt
                src = lambda lo, n, mt=mt: xsl(mt, lo, n)
                c0, ssr = stats[mt]
                par, svn = ln1_norm(mt, src, c0, ssr)
                lss[0] = ln1_transpose(par, gb * 4096 + tt * 128, svn)
            return th

        def vproj_p(tt):
            def th(tt=tt):
                mt = 4 * G + tt
                kt = mt + 1
                wv = ps_w("psp") + [(SS, lss[0])]
                if (kt - 8) in kt_last_read:
                    wv.append((ST, kt_last_read[kt - 8]))
                for n in range(2):
                    sch.op("tensor", lambda e, n=n: e.matmul(
                        psp[:, n * 512:(n + 1) * 512], onesb[:, :],
                        bvrow[0:1, n * 512:(n + 1) * 512], start=True, stop=False),
                        waits=wv if n == 0 else [])
                    for fc in range(FC):
                        stv = sch.op("tensor", lambda e, n=n, fc=fc, gb=gb, tt=tt: e.matmul(
                            psp[:, n * 512:(n + 1) * 512],
                            ln1f[:, gb * 4096 + fc * 512 + tt * 128:gb * 4096 + fc * 512 + (tt + 1) * 128],
                            winT[:, fc * I3 + 2048 + n * 512:fc * I3 + 2048 + (n + 1) * 512],
                            start=False, stop=(fc == FC - 1)),
                            inc=ST if fc == FC - 1 else None)
                psprod["psp"] = (ST, stv)
                ssv = sch.op("scalar", lambda e, kt=kt: e.copy(
                    vtb[:, vcol(kt):vcol(kt) + 1024], psp[:, :]),
                    waits=[psprod["psp"]], inc=SS)
                pscons["psp"] = (SS, ssv)
            return th

        def qk_p(oc):
            def th(oc=oc):
                kro = [kt_last_read.get(k, 0) for k in range(4 * G - 7, 4 * G - 3)]
                k_reuse = max(kro) if kro else 0
                wq = ps_w("psp") + [(SS, lss[0])]
                for fc in range(FC):
                    stv = sch.op("tensor", lambda e, oc=oc, fc=fc, gb=gb: e.matmul(
                        psp[:, 0:512],
                        winT[:, fc * I3 + oc * 128:fc * I3 + (oc + 1) * 128],
                        ln1f[:, gb * 4096 + fc * 512:gb * 4096 + (fc + 1) * 512],
                        start=(fc == 0), stop=(fc == FC - 1)),
                        waits=wq if fc == 0 else [], inc=ST if fc == FC - 1 else None)
                psprod["psp"] = (ST, stv)
                ssq = sch.op("scalar", lambda e, oc=oc, gb=gb: e.activation(
                    qf2[:, gb * 4096 + oc * 512:gb * 4096 + (oc + 1) * 512], psp[:, 0:512],
                    AF.Identity, bias=bqs[:, oc:oc + 1], scale=c18[:, :]),
                    waits=[psprod["psp"], (SV, sv_bqs)], inc=SS)
                pscons["psp"] = (SS, ssq)
                wk = ps_w("pk")
                if k_reuse:
                    wk.append((ST, k_reuse))
                for fc in range(FC):
                    stv = sch.op("tensor", lambda e, oc=oc, fc=fc, gb=gb: e.matmul(
                        pk[:, :],
                        winT[:, fc * I3 + 1024 + oc * 128:fc * I3 + 1024 + (oc + 1) * 128],
                        ln1f[:, gb * 4096 + fc * 512:gb * 4096 + (fc + 1) * 512],
                        start=(fc == 0), stop=(fc == FC - 1)),
                        waits=wk if fc == 0 else [], inc=ST if fc == FC - 1 else None)
                psprod["pk"] = (ST, stv)
                ssk = sch.op("scalar", lambda e, oc=oc, gb=gb: e.activation(
                    kf2[:, oc * 1024 + gb * 512:oc * 1024 + gb * 512 + 512], pk[:, :],
                    AF.Identity, bias=bsb[:, 8 + oc:9 + oc], scale=1.0),
                    waits=[psprod["pk"]], inc=SS)
                pscons["pk"] = (SS, ssk)
                kvr[oc] = SS.n
            return th

        P = [(1, xload(4 * G)), (1, xload(4 * G + 1)),
             (11, xload(4 * G + 2)), (19, xload(4 * G + 3, ck=True)),
             (20, stat_p(0)), (20, stat_p(1)), (21, stat_p(2)), (21, stat_p(3))]
        for tt in range(4):
            P.append((22 + tt, norm_p(tt)))
        for tt in range(4):
            P.append((26 + tt, vproj_p(tt)))
        for oc in range(FC):
            P.append((26 + oc, qk_p(oc)))
        P.sort(key=lambda t: t[0])
        return P, kvr

    P0, cur_kvr = produce_pieces(0)
    for _, th in P0:
        th()
    sskv_group = SS.n

    for G in range(4):
        gb = G % 2

        if _DEBUG.get("stop") == "kv0" and G == 0:
            sch.wait("sync", SS, sskv_group)
            for tt4 in range(4):
                sch.op("sync", lambda e, tt4=tt4: e.dma_start(
                    out=dbg_x[:, tt4 * 1024:(tt4 + 1) * 1024], in_=xsl(tt4, 0, 1024)),
                    inc=SX, inc_n=16)
            sch.wait("gpsimd", SS, sskv_group)
            sch.op("gpsimd", lambda e: e.dma_start(out=dbg_ln1f[:, :], in_=ln1f[:, :]), inc=SG, inc_n=16)
            sch.op("gpsimd", lambda e: e.dma_start(out=dbg_qf[:, :], in_=qf2[:, 0:4096]), inc=SG, inc_n=16)
            sch.op("gpsimd", lambda e: e.dma_start(out=dbg_kf[:, :], in_=kf2[:, :]), inc=SG, inc_n=16)
            sch.op("gpsimd", lambda e: e.dma_start(out=dbg_vt[:, :], in_=vtb[:, :]), inc=SG, inc_n=16)
            sch.op("sync", lambda e: e.dma_start(out=dbg_st[:, :], in_=lnstat[:, 0:8]), inc=SX, inc_n=16)
            sch.wait("sync", SX, SX.n)
            sch.wait("gpsimd", SG, SG.n)
            with nc.Block() as block:
                sch.play(block)
            esb.close()
            es0.close()
            return nc

        def stageA(u):
            qt = 4 * G + u // 8
            tt = u // 8
            hp = u % 8
            sc_ps, sc_nm = (pss, "pss") if u % 2 == 0 else (pq, "pq")
            mask_t = mfirst if qt == 0 else mband
            w0 = ps_w(sc_nm) + [(SS, cur_kvr[hp])]
            stv = None
            for hl in range(2):
                sch.op("tensor", lambda e, sc_ps=sc_ps, hl=hl, mask_t=mask_t: e.matmul(
                    sc_ps[:, hl * 256:hl * 256 + 256], identb[:, :], mask_t[:, :],
                    start=True, stop=False), waits=w0 if hl == 0 else [])
                col0 = hp * 1024 + kcol(qt)
                col1 = hp * 1024 + kcol(qt + 1)
                def qk(e, sc_ps=sc_ps, hl=hl, col0=col0, col1=col1, hp=hp, tt=tt, gb=gb):
                    rhs = bass.AP(kf2, hl * 64 * 8192 + col0,
                                  [[8192, 64], [col1 - col0, 2], [1, 128]])
                    return e.matmul(
                        sc_ps[:, hl * 256:hl * 256 + 256],
                        qf2[hl * 64:(hl + 1) * 64,
                            gb * 4096 + hp * 512 + tt * 128:gb * 4096 + hp * 512 + (tt + 1) * 128],
                        rhs, start=False, stop=True)
                stv = sch.op("tensor", qk, inc=ST if hl == 1 else None)
            kt_last_read[qt] = ST.n
            kt_last_read[qt + 1] = ST.n
            psprod[sc_nm] = (ST, stv)
            return (ST, stv)

        def stageV1(u, pa):
            sc_ps = pss if u % 2 == 0 else pq
            return sch.op("vector", lambda e, sc_ps=sc_ps, u=u % 2: e.reduce_max(
                nm2[:, u:u + 1], sc_ps[:, :], axis=AX.X, negate=True),
                waits=[pa], inc=SV)

        def stageS1(u, v1):
            sc_ps, sc_nm = (pss, "pss") if u % 2 == 0 else (pq, "pq")
            ssx = None
            for hl in range(2):
                ssx = sch.op("scalar", lambda e, sc_ps=sc_ps, hl=hl, u=u % 2: e.activation(
                    probs2[:, u * 512 + hl * 256:u * 512 + (hl + 1) * 256],
                    sc_ps[:, hl * 256:(hl + 1) * 256], AF.Exp,
                    bias=nm2[:, u:u + 1], scale=1.0,
                    accum_out=sums4[:, u * 2 + hl:u * 2 + hl + 1]),
                    waits=[(SV, v1)] if hl == 0 else [], inc=SS)
            pscons[sc_nm] = (SS, ssx)
            return ssx

        def stageV2(u, s1):
            sch.op("vector", lambda e, u=u % 2: e.reciprocal(
                rc4[:, u * 2:u * 2 + 2], sums4[:, u * 2:u * 2 + 2]),
                waits=[(SS, s1)])
            vgap()
            svd = None
            for hl in range(2):
                svd = sch.op("vector", lambda e, u=u % 2, hl=hl: e.tensor_scalar(
                    diag4[:, (u * 2 + hl) * 128:(u * 2 + hl + 1) * 128],
                    identb[:, :], rc4[:, u * 2 + hl:u * 2 + hl + 1], None, ALU.mult),
                    inc=SV)
            return svd

        def stageB(u, v2, s1):
            pt_ps, pt_nm = (pst, "pst") if u % 2 == 0 else (pk, "pk")
            wt = ps_w(pt_nm) + [(SV, v2), (SS, s1)]
            stv = None
            for j in range(4):
                hl, kc = j // 2, j % 2
                stv = sch.op("tensor",
                    lambda e, pt_ps=pt_ps, j=j, hl=hl, kc=kc, u=u % 2: e.matmul(
                        pt_ps[:, j * 128:(j + 1) * 128],
                        probs2[:, u * 512 + hl * 256 + kc * 128:u * 512 + hl * 256 + (kc + 1) * 128],
                        diag4[:, (u * 2 + hl) * 128:(u * 2 + hl + 1) * 128],
                        start=True, stop=True), waits=wt if j == 0 else [],
                    inc=ST if j == 3 else None)
            psprod[pt_nm] = (ST, stv)
            return (ST, stv)

        def stageS2(u, pb):
            pt_ps, pt_nm = (pst, "pst") if u % 2 == 0 else (pk, "pk")
            sst = sch.op("scalar", lambda e, pt_ps=pt_ps, u=u % 2: e.copy(
                probsT2[:, u * 512:(u + 1) * 512], pt_ps[:, :]),
                waits=[pb], inc=SS)
            pscons[pt_nm] = (SS, sst)
            return (SS, sst)

        def stageC(u, s2):
            qt = 4 * G + u // 8
            hp = u % 8
            po_nm = "pso" if u % 2 == 0 else "psl"
            po_t = pso if u % 2 == 0 else psl
            wo = ps_w(po_nm) + [s2]
            stv = None
            for hl in range(2):
                for kc in range(2):
                    kt = qt + kc
                    h = hp * 2 + hl
                    stv = sch.op("tensor",
                        lambda e, po_t=po_t, hl=hl, kc=kc, kt=kt, h=h, u=u % 2: e.matmul(
                            po_t[hl * 64:(hl + 1) * 64, 0:128],
                            vtb[:, vcol(kt) + h * 64:vcol(kt) + (h + 1) * 64],
                            probsT2[:, u * 512 + (hl * 2 + kc) * 128:u * 512 + (hl * 2 + kc + 1) * 128],
                            start=(kc == 0), stop=(kc == 1)),
                        waits=wo if (hl == 0 and kc == 0) else [],
                        inc=ST if (hl == 1 and kc == 1) else None)
                    kt_last_read[kt] = ST.n
            psprod[po_nm] = (ST, stv)
            return (ST, stv)

        def stageS3(u, pc):
            qt = 4 * G + u // 8
            hp = u % 8
            po_t = pso if u % 2 == 0 else psl
            po_nm = "pso" if u % 2 == 0 else "psl"
            sso = sch.op("scalar", lambda e, po_t=po_t, qt=qt, hp=hp: e.copy(
                ao2[:, (qt % 2) * 1024 + hp * 128:(qt % 2) * 1024 + (hp + 1) * 128],
                po_t[0:128, 0:128]), waits=[pc], inc=SS)
            pscons[po_nm] = (SS, sso)
            return sso

        def finish_qt(qt):
            wv = ps_w("psp") + [(SS, SS.n), (SG, sg_weights)]
            for n in range(2):
                sch.op("tensor", lambda e, n=n: e.matmul(
                    psp[:, n * 512:(n + 1) * 512], onesb[:, :],
                    borow[0:1, n * 512:(n + 1) * 512], start=True, stop=False),
                    waits=wv if n == 0 else [])
                for fc in range(FC):
                    stv = sch.op("tensor", lambda e, n=n, fc=fc, qt=qt: e.matmul(
                        psp[:, n * 512:(n + 1) * 512],
                        ao2[:, (qt % 2) * 1024 + fc * 128:(qt % 2) * 1024 + (fc + 1) * 128],
                        woutT[:, fc * 1024 + n * 512:fc * 1024 + (n + 1) * 512],
                        start=False, stop=(fc == FC - 1)), inc=ST if fc == FC - 1 else None)
            psprod["psp"] = (ST, stv)
            wstt = [psprod["psp"]]
            if qt >= 2:
                wstt.append((SYW, qt - 1))
            svx = None
            for n in range(2):
                svx = sch.op("vector", lambda e, n=n, qt=qt: e.scalar_tensor_tensor(
                    x1t[:, (qt % 2) * 1024 + n * 512:(qt % 2) * 1024 + (n + 1) * 512],
                    psp[:, n * 512:(n + 1) * 512], 1.0,
                    xsl(qt, n * 512, 512), ALU.mult, ALU.add,
                    accum_out=x1s[:, n:n + 1]),
                    waits=wstt if n == 0 else [], inc=SV)
            pscons["psp"] = (SV, svx)
            x_last_read[qt] = (SV, svx)
            if qt >= 1:
                sch.wait("sync", SX, y_write_sx[qt - 1])
                sch.op("sync", lambda e: e.nop(), inc=SYW)
            sch.op("sync", lambda e, qt=qt: e.dma_start(
                out=dy[qt * 128:(qt + 1) * 128, :],
                in_=x1t[:, (qt % 2) * 1024:(qt % 2 + 1) * 1024]),
                waits=[(SV, svx)], inc=SX, inc_n=16)
            y_write_sx[qt] = SX.n

        def ln2_a(qt):
            q0 = (qt % 2) * 8
            vgap()
            sch.op("vector", lambda e, q0=q0: e.tensor_tensor(
                qstat[:, q0:q0 + 1], x1s[:, 0:1], x1s[:, 1:2], ALU.add))
            sch.op("vector", lambda e, qt=qt, q0=q0: e.scalar_tensor_tensor(
                scr[:, :], x1t[:, (qt % 2) * 1024:(qt % 2 + 1) * 1024], 1.0,
                x1t[:, (qt % 2) * 1024:(qt % 2 + 1) * 1024],
                ALU.mult, ALU.mult, accum_out=qstat[:, q0 + 1:q0 + 2]))
            sch.op("vector", lambda e, qt=qt, q0=q0: e.tensor_scalar(
                m_all[:, qt:qt + 1], qstat[:, q0:q0 + 1], 1.0 / 1024.0, None, ALU.mult))

        def ln2_b(qt):
            q0 = (qt % 2) * 8
            sch.op("vector", lambda e, qt=qt, q0=q0: e.reduce_max(
                qstat[:, q0 + 6:q0 + 7], x1t[:, (qt % 2) * 1024:(qt % 2 + 1) * 1024], axis=AX.X))

        def ln2_c(qt):
            q0 = (qt % 2) * 8
            sch.op("vector", lambda e, qt=qt, q0=q0: e.tensor_reduce(
                qstat[:, q0 + 7:q0 + 8], x1t[:, (qt % 2) * 1024:(qt % 2 + 1) * 1024],
                axis=AX.X, op=ALU.min))

        def ln2_d(qt):
            q0 = (qt % 2) * 8
            sch.op("vector", lambda e, q0=q0: e.tensor_scalar(
                qstat[:, q0 + 2:q0 + 3], qstat[:, q0 + 1:q0 + 2], 1.0 / 1024.0, None, ALU.mult))
            sch.op("vector", lambda e, qt=qt, q0=q0: e.tensor_tensor(
                qstat[:, q0 + 3:q0 + 4], m_all[:, qt:qt + 1], m_all[:, qt:qt + 1], ALU.mult))
            vgap()
            svv = sch.op("vector", lambda e, q0=q0: e.tensor_tensor(
                qstat[:, q0 + 4:q0 + 5], qstat[:, q0 + 2:q0 + 3],
                qstat[:, q0 + 3:q0 + 4], ALU.subtract), inc=SV)
            ssl = sch.op("scalar", lambda e, q0=q0: e.activation(
                qstat[:, q0 + 5:q0 + 6], qstat[:, q0 + 4:q0 + 5], AF.Ln,
                bias=cEPS[:, :], scale=1.0), waits=[(SV, svv)], inc=SS)
            ssr = sch.op("scalar", lambda e, qt=qt, q0=q0: e.activation(
                rstd_all[:, qt:qt + 1], qstat[:, q0 + 5:q0 + 6], AF.Exp,
                bias=0.0, scale=cN05[:, :]), waits=[(SS, ssl)], inc=SS)
            sch.op("vector", lambda e, qt=qt, q0=q0: e.tensor_tensor(
                qstat[:, q0 + 6:q0 + 7], qstat[:, q0 + 6:q0 + 7], m_all[:, qt:qt + 1], ALU.subtract))
            sch.op("vector", lambda e, qt=qt, q0=q0: e.tensor_tensor(
                qstat[:, q0 + 7:q0 + 8], m_all[:, qt:qt + 1], qstat[:, q0 + 7:q0 + 8], ALU.subtract))
            vgap()
            sch.op("vector", lambda e, q0=q0: e.tensor_tensor(
                qstat[:, q0 + 6:q0 + 7], qstat[:, q0 + 6:q0 + 7], qstat[:, q0 + 7:q0 + 8], ALU.max))
            vgap()
            sch.op("vector", lambda e, qt=qt, q0=q0: e.tensor_tensor(
                qstat[:, q0 + 6:q0 + 7], qstat[:, q0 + 6:q0 + 7], rstd_all[:, qt:qt + 1], ALU.mult),
                waits=[(SS, ssr)])
            vgap()
            sch.op("vector", lambda e, q0=q0: e.tensor_tensor(
                absx[:, :], absx[:, :], qstat[:, q0 + 6:q0 + 7], ALU.max))

        NU = 32
        H = {}
        pieces, kvr_next = produce_pieces(G + 1) if G < 3 else ([], None)
        ppi = 0
        pend_ln2 = {}
        for i in range(NU + 8):
            if i < NU:
                pa = stageA(i)
                v1 = stageV1(i, pa)
                H[i] = {"pa": pa, "v1": v1}
            u = i - 1
            if 0 <= u < NU:
                ws = []
                if (u - 2) in H and "pb" in H[u - 2]:
                    ws.append(H[u - 2]["pb"])
                if (u - 2) in H and "v2" in H[u - 2]:
                    ws.append((SV, H[u - 2]["v2"]))
                for w_ in ws:
                    sch.wait("scalar", w_[0], w_[1])
                H[u]["s1"] = stageS1(u, H[u]["v1"])
            u = i - 2
            if 0 <= u < NU:
                if (u - 2) in H and "pb" in H[u - 2]:
                    sch.wait("vector", H[u - 2]["pb"][0], H[u - 2]["pb"][1])
                H[u]["v2"] = stageV2(u, H[u]["s1"])
                H[u]["pb"] = stageB(u, H[u]["v2"], H[u]["s1"])
                ws = []
                if (u - 2) in H and "pc" in H[u - 2]:
                    sch.wait("scalar", H[u - 2]["pc"][0], H[u - 2]["pc"][1])
                H[u]["s2"] = stageS2(u, H[u]["pb"])
            u = i - 3
            if 0 <= u < NU:
                H[u]["pc"] = stageC(u, H[u]["s2"])
                stageS3(u, H[u]["pc"])
                if u % 8 == 7:
                    finish_qt(4 * G + u // 8)
                    qq = 4 * G + u // 8
                    pend_ln2[i + 2] = (ln2_a, qq)
                    pend_ln2[i + 3] = (ln2_b, qq)
                    pend_ln2[i + 4] = (ln2_c, qq)
                    pend_ln2[i + 5] = (ln2_d, qq)
            if i in pend_ln2:
                fn, qq = pend_ln2.pop(i)
                fn(qq)
            while ppi < len(pieces) and pieces[ppi][0] <= i:
                pieces[ppi][1]()
                ppi += 1
        assert ppi == len(pieces)
        sskv_group = SS.n
        cur_kvr = kvr_next

    stx = sch.op("tensor", lambda e: e.matmul(
        psl[0:1, 0:128], absx[:, :], identf[:, :], start=True, stop=True),
        waits=ps_w("psl") + [(SV, SV.n)], inc=ST)
    psprod["psl"] = (ST, stx)
    svr = sch.op("vector", lambda e: e.tensor_copy(ccrow[0:1, :], psl[0:1, 0:128]),
                 waits=[psprod["psl"]], inc=SV)
    pscons["psl"] = (SV, svr)

    if _DEBUG.get("stop") == "x1":
        sch.wait("sync", SX, SX.n)
        sch.wait("gpsimd", SG, SG.n)
        with nc.Block() as block:
            sch.play(block)
        esb.close()
        es0.close()
        return nc

    if _DEBUG.get("scales"):
        sch.op("vector", lambda e: e.tensor_copy(t11[0:1, 2:3], t11[0:1, 2:3]))
        sch.op("sync", lambda e: e.dma_start(out=dbg_sc[0:1, 0:1], in_=rxv[:, :]),
               waits=[(SV, SV.n)], inc=SX, inc_n=16)
        sch.op("sync", lambda e: e.dma_start(out=dbg_sc[0:1, 1:2], in_=sxv[:, :]), inc=SX, inc_n=16)
        sch.op("sync", lambda e: e.dma_start(out=dbg_sc[0:1, 2:3], in_=sigx[:, :]), inc=SX, inc_n=16)
        sch.op("sync", lambda e: e.dma_start(out=dbg_sc[0:1, 3:4], in_=sigd[:, :]), inc=SX, inc_n=16)

    with nc.Block() as block:
        sch.play(block)
    esb.close()

    esc = ExitStack()

    def sbC(name, shape, dtype):
        return esc.enter_context(nc.sbuf_tensor(name, list(shape), dtype))

    g2 = sbC("g2", [128, IC * 2048], BF16)
    qwdTa = sbC("qwdTa", [128, IC * 512], BF16)
    srow = sbC("srow", [1, E], F32)
    wdn = sbC("wdn", [128, II], F32)
    rmd = sbC("rmd", [128, 8], F32)
    cgb = sbC("cgb", [128, 4], F32)
    cub = sbC("cub", [128, 4], F32)
    rwb = sbC("rwb", [128, 8], F32)
    rmg = sbC("rmg", [128, 8], F32)
    aqb = sbC("aqb", [128, 4], F32)

    esct = ExitStack()

    def sbT(name, shape, dtype):
        return esct.enter_context(nc.sbuf_tensor(name, list(shape), dtype))

    xqf = sbT("xqf", [128, 8 * 2048], BF16)
    qwg = sbT("qwg", [128, 4 * 8 * 128], BF16)
    qwu = sbT("qwu", [128, 4 * 8 * 128], BF16)
    tmq = sbT("tmq", [128, 1024], BF16)
    silu2 = sbT("silu2", [128, 2 * 512], F32)
    wgn = sbT("wgn", [128, 4 * 1024], F32)

    tmq_lastT = [0]
    rw_lastS = {i: (SS, 0) for i in range(8)}
    qslot_ctr = [0]

    sch.wait("sync", SX, y_write_sx[15])
    sch.op("sync", lambda e: e.nop(), inc=SYW)
    sch.wait("gpsimd", SV, svr)
    sch.op("gpsimd", lambda e: e.dma_start(out=cc_in_x[:, :], in_=ccrow[0:1, :]),
           inc=SG, inc_n=16)
    sch.wait("gpsimd", SG, SG.n)
    scc1 = sch.op("gpsimd", lambda e: e.collective_compute(
        "AllReduce", ALU.max, replica_groups=[list(range(8))],
        ins=[cc_in_x.ap()], outs=[cc_out_x.ap()]), inc=SCC)
    sch.wait("gpsimd", SCC, scc1)
    sch.op("gpsimd", lambda e: e.dma_start(out=ccrow2[0:1, :], in_=cc_out_x[:, :]),
           inc=SG, inc_n=16)
    sgw_b1 = ckpt("gpsimd", SG, SGW)
    sch.op("scalar", load_table(SET_SILU))

    def quant_chunk(nat_col, ncols, rw_slot, rowmax_ap, ready_w, write_piece,
                    evac_w=(), scrb=None, piece_ss=None, pre=None):
        if pre is None:
            rw_slot = 6 + qslot_ctr[0] % 2
            qslot_ctr[0] += 1
            svm = sch.op("vector", lambda e: e.reduce_max(
                rowmax_ap, (wgn if ncols == 1024 else wdn)[:, nat_col:nat_col + ncols],
                axis=AX.X, apply_absolute_value=True),
                waits=list(ready_w) + [rw_lastS[rw_slot]], inc=SV)
            vgap()
            sch.op("vector", lambda e: e.reciprocal(tmpv[:, 0:1], rowmax_ap))
            vgap()
            svw = sch.op("vector", lambda e, rw_slot=rw_slot: e.tensor_scalar(
                rwb[:, rw_slot:rw_slot + 1], tmpv[:, 0:1],
                127.0, None, ALU.mult), inc=SV)
        else:
            rw_slot, svw = pre
            svm = 0
        ss_last = 0
        svq = 0
        if scrb is None:
            scrb = scr
        svw = max(svw, scr_free[0])
        gf = 0
        for p0 in range(0, ncols, 1024):
            pn = min(1024, ncols - p0)
            wq = [(SV, svw)] if svq == 0 else [(SS, svq)]
            ssq = sch.op("scalar", lambda e, p0=p0, pn=pn, nat_col=nat_col, rw_slot=rw_slot, ncols=ncols, scrb=scrb: e.activation(
                scrb[:, 0:pn], (wgn if ncols == 1024 else wdn)[:, nat_col + p0:nat_col + p0 + pn],
                AF.Identity, bias=cMAG[:, :], scale=rwb[:, rw_slot:rw_slot + 1]),
                waits=wq, inc=SS)
            if piece_ss is not None:
                piece_ss[p0] = ssq
            svq = sch.op("scalar", lambda e, pn=pn, scrb=scrb: e.activation(
                tmq[:, 0:pn], scrb[:, 0:pn], AF.Identity,
                bias=cNMG[:, :], scale=1.0),
                waits=[(SS, ssq), (ST, tmq_lastT[0])], inc=SS)
            npieces = pn // 128
            for g0 in range(0, npieces, 4):
                gn = min(4, npieces - g0)
                ps, name = (psl, "psl") if gf % 2 == 0 else (pss, "pss")
                gf += 1
                wt = ps_w(name) + [(SS, svq)]
                for c in range(g0, g0 + gn):
                    stv = sch.op("tensor", lambda e, ps=ps, c=c, g0=g0: e.matmul(
                        ps[:, (c - g0) * 128:(c - g0 + 1) * 128],
                        tmq[:, c * 128:(c + 1) * 128], identb[:, :],
                        start=True, stop=True),
                        waits=wt if c == g0 else [],
                        inc=ST if c == g0 + gn - 1 else None)
                psprod[name] = (ST, stv)
                tmq_lastT[0] = stv
                ss_last = sch.op("vector",
                                 lambda e, ps=ps, p0=p0, g0=g0, gn=gn: write_piece(
                                     e, ps, p0 // 128 + g0, gn),
                                 waits=[psprod[name]] + list(evac_w), inc=SV)
                pscons[name] = (SV, ss_last)
        rw_lastS[rw_slot] = (SV, ss_last)
        return (SV, ss_last), svm

    wgn_slot_reads = {}
    scr_free = [0]
    wd_scr = [None]
    wd_sched = {14: 0, 16: 1, 18: 2, 20: 3}
    swd_prev = [0]
    qwdTa_ss = (SS, 0)

    wd_piece_ss = {}

    def wd_issue(dc):
        ready = []
        for i, p0 in enumerate((0, 1024, 2048)):
            pn = min(1024, II - p0)
            w = [(SS, wd_piece_ss[p0])] if p0 in wd_piece_ss else [(SYW, 16)]
            sch.op("sync", lambda e, dc=dc, p0=p0, pn=pn: e.dma_start(
                out=wdn[:, p0:p0 + pn], in_=dwd[dc * 128:(dc + 1) * 128, p0:p0 + pn]),
                waits=w, inc=SDP[i], inc_n=16)
            ready.append((SDP[i], SDP[i].n))
        return ready

    def wd_quant(dc, dst_sb, dst_dcl, ready=None):
        if ready is None:
            ready = wd_issue(dc)

        def wp(e, ps, piece, gn, dst_sb=dst_sb, dst_dcl=dst_dcl):
            src = bass.AP(ps, 0, [[512, 128], [128, gn], [1, 128]])
            dst = bass.AP(dst_sb, piece * 512 + dst_dcl * 128,
                          [[IC * 512, 128], [512, gn], [1, 128]])
            return e.tensor_copy(dst, src)
        ss_last, svm = quant_chunk(0, II, 0, rmd[:, dc:dc + 1], ready, wp,
                                   scrb=wd_scr[0], piece_ss=wd_piece_ss)
        stv = sch.op("tensor", lambda e, dc=dc: e.matmul(
            pso[0:1, 0:128], rmd[:, dc:dc + 1], identf[:, :], start=True, stop=True),
            waits=ps_w("pso") + [(SV, svm)], inc=ST)
        psprod["pso"] = (ST, stv)
        svsr = sch.op("vector", lambda e, dc=dc: e.tensor_copy(
            srow[0:1, dc * 128:(dc + 1) * 128], pso[0:1, 0:128]),
            waits=[psprod["pso"]], inc=SV)
        pscons["pso"] = (SV, svsr)
        swd_prev[0] = ss_last
        return ss_last

    swp_val = {}
    pair_pre = {}

    def issue_pair(oc):
        gslot = (2 * oc) % 4
        uslot = (2 * oc + 1) % 4
        wg_w = [wgn_slot_reads[gslot]] if gslot in wgn_slot_reads else [(SYW, 16)]
        sch.op("sync", lambda e, oc=oc, gslot=gslot: e.dma_start(
            out=wgn[:, gslot * 1024:(gslot + 1) * 1024],
            in_=dwg[oc * 128:(oc + 1) * 128, :]), waits=wg_w, inc=SWP[gslot], inc_n=16)
        swp_val[gslot] = SWP[gslot].n
        wu_w = [wgn_slot_reads[uslot]] if uslot in wgn_slot_reads else [(SYW, 16)]
        sch.op("sync", lambda e, oc=oc, uslot=uslot: e.dma_start(
            out=wgn[:, uslot * 1024:(uslot + 1) * 1024],
            in_=dwu[oc * 128:(oc + 1) * 128, :]), waits=wu_w, inc=SWP[uslot], inc_n=16)
        swp_val[uslot] = SWP[uslot].n
        par = oc % 4
        pre = {}
        for which, slot_ld, rs in (("g", gslot, par * 2), ("u", uslot, par * 2 + 1)):
            svm = sch.op("vector", lambda e, slot_ld=slot_ld, rs=rs: e.reduce_max(
                rmg[:, rs:rs + 1], wgn[:, slot_ld * 1024:(slot_ld + 1) * 1024],
                axis=AX.X, apply_absolute_value=True),
                waits=[(SWP[slot_ld], swp_val[slot_ld]), rw_lastS[rs]], inc=SV)
            vgap()
            sch.op("vector", lambda e, rs=rs: e.reciprocal(
                tmpv[:, 0:1], rmg[:, rs:rs + 1]))
            vgap()
            svw = sch.op("vector", lambda e, rs=rs: e.tensor_scalar(
                rwb[:, rs:rs + 1], tmpv[:, 0:1], 127.0, None, ALU.mult), inc=SV)
            pre[which] = (rs, svw)
        pair_pre[oc] = pre

    sv_absg = 0
    prev_svg = {0: 0, 1: 0}
    qw_lastT = {0: 0, 1: 0, 2: 0, 3: 0}

    def quant_gate(oc):
        gslot = (2 * oc) % 4
        par = oc % 4
        evw = [(ST, qw_lastT[par])] if qw_lastT[par] else []
        pre = pair_pre.pop(oc)

        def wpg(e, ps, piece, gn, par=par):
            return e.tensor_copy(
                qwg[:, par * 1024 + piece * 128:par * 1024 + (piece + gn) * 128],
                ps[:, 0:gn * 128])
        ssg, _ = quant_chunk(gslot * 1024, 1024, 0, None, [], wpg,
                             evac_w=evw, pre=pre["g"])
        wgn_slot_reads[gslot] = ssg
        return dict(ssg=ssg, par=par, pre=pre, evw=evw, oc=oc)

    def quant_up(q):
        oc = q["oc"]
        par = q["par"]
        uslot = (2 * oc + 1) % 4

        def wpu(e, ps, piece, gn, par=par):
            return e.tensor_copy(
                qwu[:, par * 1024 + piece * 128:par * 1024 + (piece + gn) * 128],
                ps[:, 0:gn * 128])
        ssu, _ = quant_chunk(uslot * 1024, 1024, 1, None, [], wpu,
                             evac_w=q["evw"], pre=q["pre"]["u"])
        wgn_slot_reads[uslot] = ssu
        q["ssu"] = ssu
        return q

    def quant_pair(oc):
        return quant_up(quant_gate(oc))

    def pair_scales(q):
        par = q["par"]
        q["svcg"] = sch.op("vector", lambda e, par=par: e.scalar_tensor_tensor(
            cgb[:, par:par + 1], rmg[:, par * 2:par * 2 + 1], 1.0 / 127.0, sxb[:, :],
            ALU.mult, ALU.mult), waits=[(SV, sv_rxb)], inc=SV)
        q["svcu"] = sch.op("vector", lambda e, par=par: e.scalar_tensor_tensor(
            cub[:, par:par + 1], rmg[:, par * 2 + 1:par * 2 + 2], 1.0 / 127.0, sxb[:, :],
            ALU.mult, ALU.mult), inc=SV)
        vgap()

    def gate_up_unit(oc, Gt, q):
        par = q["par"]
        u = (oc * 4 + Gt) % 2
        gps, gnm = (pq, "pq") if u == 0 else (pst, "pst")
        ups, unm = (pk, "pk") if u == 0 else (psp, "psp")
        wg2 = ps_w(gnm) + [q["ssg"], (SS, ss_xq_done)]
        for fcx in range(FC):
            stv = sch.op("tensor", lambda e, gps=gps, fcx=fcx, par=par, Gt=Gt: e.matmul(
                gps[:, :] if gps is not psp else gps[:, 0:512],
                qwg[:, par * 1024 + fcx * 128:par * 1024 + (fcx + 1) * 128],
                xqf[:, fcx * 2048 + Gt * 512:fcx * 2048 + (Gt + 1) * 512],
                start=(fcx == 0), stop=(fcx == FC - 1)),
                waits=wg2 if fcx == 0 else [], inc=ST if fcx == FC - 1 else None)
        psprod[gnm] = (ST, stv)
        wu2 = ps_w(unm) + [q["ssu"]]
        for fcx in range(FC):
            stv = sch.op("tensor", lambda e, ups=ups, fcx=fcx, par=par, Gt=Gt: e.matmul(
                ups[:, :] if ups is not psp else ups[:, 0:512],
                qwu[:, par * 1024 + fcx * 128:par * 1024 + (fcx + 1) * 128],
                xqf[:, fcx * 2048 + Gt * 512:fcx * 2048 + (Gt + 1) * 512],
                start=(fcx == 0), stop=(fcx == FC - 1)),
                waits=wu2 if fcx == 0 else [], inc=ST if fcx == FC - 1 else None)
        psprod[unm] = (ST, stv)
        qw_lastT[par] = stv
        sil = sch.op("scalar", lambda e, gps=gps, u=u, par=par: e.activation(
            silu2[:, u * 512:(u + 1) * 512],
            gps[:, :] if gps is not psp else gps[:, 0:512], AF.Silu,
            bias=0.0, scale=cgb[:, par:par + 1]),
            waits=[psprod[gnm], (SV, q["svcg"]), (SV, prev_svg[u])], inc=SS)
        pscons[gnm] = (SS, sil)
        svg = sch.op("vector", lambda e, ups=ups, u=u, par=par, oc=oc, Gt=Gt: e.scalar_tensor_tensor(
            g2[:, oc * 2048 + Gt * 512:oc * 2048 + (Gt + 1) * 512],
            ups[:, :] if ups is not psp else ups[:, 0:512],
            cub[:, par:par + 1], silu2[:, u * 512:(u + 1) * 512],
            ALU.mult, ALU.mult),
            waits=[psprod[unm], (SS, sil), (SV, q["svcu"])], inc=SV)
        pscons[unm] = (SV, svg)
        prev_svg[u] = svg
        vgap()
        sch.op("vector", lambda e, oc=oc, Gt=Gt: e.reduce_max(
            tmpv[:, 1:2], g2[:, oc * 2048 + Gt * 512:oc * 2048 + (Gt + 1) * 512],
            axis=AX.X, apply_absolute_value=True))
        vgap()
        return sch.op("vector", lambda e: e.tensor_tensor(
            absg[:, :], absg[:, :], tmpv[:, 1:2], ALU.max), inc=SV)

    QP = {}
    nextq = [0]
    nextl = [1]

    def advance_quant():
        if nextq[0] < IC:
            q = quant_pair(nextq[0])
            pair_scales(q)
            QP[nextq[0]] = q
            nextq[0] += 1
        if nextl[0] < IC:
            issue_pair(nextl[0])
            nextl[0] += 1

    issue_pair(0)
    qwdTa_ss = wd_quant(0, qwdTa, 0)
    issue_pair(1)
    QP[0] = quant_pair(0)
    qwdTa_ss = wd_quant(1, qwdTa, 1)
    issue_pair(2)
    QP[1] = quant_pair(1)
    qwdTa_ss = wd_quant(2, qwdTa, 2)
    QP[2] = quant_pair(2)
    qwdTa_ss = wd_quant(3, qwdTa, 3)
    nextq[0] = 3
    nextl[0] = 4

    sch.op("vector", lambda e: e.reduce_max(t11[0:1, 2:3], ccrow2[0:1, :], axis=AX.X),
           waits=[(SGW, sgw_b1), (SV, sv_sig)])
    vgap()
    sch.op("vector", lambda e: e.tensor_tensor(t11[0:1, 2:3], t11[0:1, 2:3], sigx[:, :], ALU.mult))
    vgap()
    sch.op("vector", lambda e: e.reciprocal(t11[0:1, 3:4], t11[0:1, 2:3]))
    vgap()
    sch.op("vector", lambda e: e.tensor_scalar(rxv[:, :], t11[0:1, 3:4], 127.0, None, ALU.mult))
    svsc = sch.op("vector", lambda e: e.tensor_scalar(
        sxv[:, :], t11[0:1, 2:3], 1.0 / 127.0, None, ALU.mult), inc=SV)
    sch.op("tensor", lambda e: e.matmul(
        pso[:, 0:1], onesf[:, :], rxv[0:1, 0:1], start=True, stop=True),
        waits=ps_w("pso") + [(SV, svsc)])
    stb2 = sch.op("tensor", lambda e: e.matmul(
        pso[:, 1:2], onesf[:, :], sxv[0:1, 0:1], start=True, stop=True), inc=ST)
    psprod["pso"] = (ST, stb2)
    sch.op("vector", lambda e: e.tensor_copy(rxb[:, :], pso[:, 0:1]), waits=[psprod["pso"]])
    sv_rxb = sch.op("vector", lambda e: e.tensor_copy(sxb[:, :], pso[:, 1:2]), inc=SV)
    pscons["pso"] = (SV, sv_rxb)
    pair_scales(QP[0])
    pair_scales(QP[1])
    pair_scales(QP[2])
    issue_pair(3)

    ss_prequant = SS.n
    xq_read_sv = {}
    xqbuf = [scr, silu2]
    for qt in range(NQT):
        par = qt % 2
        xb = xqbuf[par]
        w = [(SYW, 16)]
        if par == 0:
            w.append((SS, ss_prequant))
        if (qt - 2) in xq_read_sv:
            w.append((SV, xq_read_sv[qt - 2]))
        sch.op("sync", lambda e, qt=qt, xb=xb: e.dma_start(
            out=xb[:, :], in_=dy[qt * 128:(qt + 1) * 128, :]),
            waits=w, inc=SX, inc_n=16)
        sx_r = SX.n
        a0 = par * 2
        sch.op("vector", lambda e, qt=qt, a0=a0: e.tensor_tensor(
            aqb[:, a0:a0 + 1], rstd_all[:, qt:qt + 1], rxb[:, :], ALU.mult))
        vgap()
        sch.op("vector", lambda e, qt=qt, a0=a0: e.tensor_tensor(
            aqb[:, a0 + 1:a0 + 2], m_all[:, qt:qt + 1], aqb[:, a0:a0 + 1], ALU.mult))
        vgap()
        svab = sch.op("vector", lambda e, a0=a0: e.tensor_scalar(
            aqb[:, a0 + 1:a0 + 2], aqb[:, a0 + 1:a0 + 2], -1.0, CMAGIC,
            ALU.mult, ALU.add), inc=SV)
        ssq = sch.op("scalar", lambda e, a0=a0, xb=xb: e.activation(
            xb[:, :], xb[:, :],
            AF.Identity, bias=aqb[:, a0 + 1:a0 + 2], scale=aqb[:, a0:a0 + 1]),
            waits=[(SX, sx_r), (SV, svab)], inc=SS)
        sch.op("vector", lambda e, xb=xb: e.tensor_scalar(
            xb[:, :], xb[:, :],
            -CMAGIC, 127.0, ALU.add, ALU.min),
            waits=[(SS, ssq), (ST, tmq_lastT[0])])
        vgap()
        svq = sch.op("vector", lambda e, xb=xb: e.tensor_scalar(
            tmq[:, :], xb[:, :], -128.0, None, ALU.max), inc=SV)
        xq_read_sv[qt] = svq
        for half in range(2):
            ps, name = (psl, "psl") if half == 0 else (pss, "pss")
            wt = ps_w(name) + [(SV, svq)]
            for c4 in range(4):
                c = half * 4 + c4
                stv = sch.op("tensor", lambda e, ps=ps, c4=c4, c=c: e.matmul(
                    ps[:, c4 * 128:(c4 + 1) * 128],
                    tmq[:, c * 128:(c + 1) * 128], identb[:, :],
                    start=True, stop=True),
                    waits=wt if c4 == 0 else [], inc=ST if c4 == 3 else None)
            psprod[name] = (ST, stv)
            tmq_lastT[0] = stv

            def evq(e, ps=ps, qt=qt, half=half):
                src = bass.AP(ps, 0, [[512, 128], [128, 4], [1, 128]])
                dst = bass.AP(xqf, half * 4 * 2048 + qt * 128,
                              [[16384, 128], [2048, 4], [1, 128]])
                return e.copy(dst, src)
            ssv = sch.op("scalar", evq, waits=[psprod[name]], inc=SS)
            pscons[name] = (SS, ssv)
    ss_xq_done = SS.n
    scr_free[0] = xq_read_sv[14]

    if _DEBUG.get("stop") == "xq":
        sch.wait("gpsimd", SS, ss_xq_done)
        sch.op("gpsimd", lambda e: e.dma_start(out=dbg_xq[:, :], in_=xqf[:, :]), inc=SG, inc_n=16)
        sch.wait("gpsimd", SG, SG.n)
        sch.wait("sync", SX, SX.n)
        with nc.Block() as block:
            sch.play(block)
        esct.close()
        esc.close()
        es0.close()
        return nc


    for oc in range(IC):
        adv = nextq[0] <= min(oc + 3, IC - 1)
        if adv:
            k = nextq[0]
            qpart = quant_gate(k)
        q = QP.pop(oc)
        sv_absg = gate_up_unit(oc, 0, q)
        if adv:
            quant_up(qpart)
            pair_scales(qpart)
            QP[k] = qpart
            nextq[0] += 1
        sv_absg = gate_up_unit(oc, 1, q)
        if adv and nextl[0] < IC:
            issue_pair(nextl[0])
            nextl[0] += 1
        sv_absg = gate_up_unit(oc, 2, q)
        sv_absg = gate_up_unit(oc, 3, q)
    wd4_ready = wd_issue(4)

    if _DEBUG.get("stop") == "g":
        sch.wait("gpsimd", SV, sv_absg)
        sch.op("gpsimd", lambda e: e.dma_start(out=dbg_g[:, :], in_=g2[:, :]), inc=SG, inc_n=16)
        sch.wait("gpsimd", SG, SG.n)
        sch.wait("sync", SX, SX.n)
        with nc.Block() as block:
            sch.play(block)
        esct.close()
        esc.close()
        es0.close()
        return nc

    with nc.Block() as block:
        sch.play(block)
    esct.close()

    esd = ExitStack()
    qwdTb = esd.enter_context(nc.sbuf_tensor("qwdTb", [128, IC * 512], BF16))
    otile = esd.enter_context(nc.sbuf_tensor("otile", [128, 2 * 512], F32))
    tmq2 = esd.enter_context(nc.sbuf_tensor("tmq2", [128, 1024], BF16))
    sdb = esd.enter_context(nc.sbuf_tensor("sdb", [128, E], F32))
    scrD = esd.enter_context(nc.sbuf_tensor("scrD", [128, 1024], F32))
    tmq = tmq2
    tmq_lastT[0] = 0
    wd_scr[0] = scrD

    stx = sch.op("tensor", lambda e: e.matmul(
        psl[0:1, 0:128], absg[:, :], identf[:, :], start=True, stop=True),
        waits=ps_w("psl") + [(SV, sv_absg)], inc=ST)
    psprod["psl"] = (ST, stx)
    svr = sch.op("vector", lambda e: e.tensor_copy(ccrow[0:1, :], psl[0:1, 0:128]),
                 waits=[psprod["psl"]], inc=SV)
    pscons["psl"] = (SV, svr)
    sch.wait("gpsimd", SV, svr)
    sch.op("gpsimd", lambda e: e.dma_start(out=cc_in_g[:, :], in_=ccrow[0:1, :]),
           inc=SG, inc_n=16)
    sch.wait("gpsimd", SG, SG.n)
    scc2 = sch.op("gpsimd", lambda e: e.collective_compute(
        "AllReduce", ALU.max, replica_groups=[list(range(8))],
        ins=[cc_in_g.ap()], outs=[cc_out_g.ap()]), inc=SCC)
    sch.wait("gpsimd", SCC, scc2)
    sch.op("gpsimd", lambda e: e.dma_start(out=ccrow2[0:1, :], in_=cc_out_g[:, :]),
           inc=SG, inc_n=16)
    sgw_b2 = ckpt("gpsimd", SG, SGW)
    qwdTb_ss = wd_quant(4, qwdTb, 0, ready=wd4_ready)
    qwdTb_ss = wd_quant(5, qwdTb, 1)
    sch.op("vector", lambda e: e.reduce_max(t11[0:1, 2:3], ccrow2[0:1, :], axis=AX.X),
           waits=[(SGW, sgw_b2)])
    vgap()
    sch.op("vector", lambda e: e.tensor_tensor(t11[0:1, 2:3], t11[0:1, 2:3], sigd[:, :], ALU.mult))
    vgap()
    sch.op("vector", lambda e: e.reciprocal(t11[0:1, 3:4], t11[0:1, 2:3]))
    vgap()
    sch.op("vector", lambda e: e.tensor_scalar(rhv[:, :], t11[0:1, 3:4], 127.0, None, ALU.mult))
    svsc2 = sch.op("vector", lambda e: e.tensor_scalar(
        shv[:, :], t11[0:1, 2:3], 1.0 / (127.0 * 127.0), None, ALU.mult), inc=SV)
    sch.op("tensor", lambda e: e.matmul(
        pso[:, 0:1], onesf[:, :], rhv[0:1, 0:1], start=True, stop=True),
        waits=ps_w("pso") + [(SV, svsc2)])
    stb = sch.op("tensor", lambda e: e.matmul(
        pso[:, 2:3], onesf[:, :], shv[0:1, 0:1], start=True, stop=True), inc=ST)
    psprod["pso"] = (ST, stb)
    sch.op("vector", lambda e: e.tensor_copy(rhb[:, :], pso[:, 0:1]),
           waits=[psprod["pso"]])
    svb2 = sch.op("vector", lambda e: e.tensor_copy(shb[:, :], pso[:, 2:3]), inc=SV)
    vgap()
    pscons["pso"] = (SV, svb2)

    if _DEBUG.get("scales"):
        sch.op("sync", lambda e: e.dma_start(out=dbg_sc[0:1, 4:5], in_=rhv[:, :]),
               waits=[(SV, SV.n)], inc=SX, inc_n=16)
        sch.op("sync", lambda e: e.dma_start(out=dbg_sc[0:1, 5:6], in_=shv[:, :]), inc=SX, inc_n=16)
        sch.op("sync", lambda e: e.dma_start(out=dbg_sc[0:1, 6:7], in_=absx[0:1, 0:1]), inc=SX, inc_n=16)
        sch.op("sync", lambda e: e.dma_start(out=dbg_sc[0:1, 7:8], in_=absg[0:1, 0:1]), inc=SX, inc_n=16)

    stv = sch.op("tensor", lambda e: e.matmul(
        pst[:, :], onesf[:, :], srow[0:1, 0:512], start=True, stop=True),
        waits=ps_w("pst") + [(SV, svb2)], inc=ST)
    psprod["pst"] = (ST, stv)
    sv_sdb = sch.op("vector", lambda e: e.tensor_scalar(
        sdb[:, 0:512], pst[:, :], shb[:, :], None, ALU.mult),
        waits=[psprod["pst"]], inc=SV)
    pscons["pst"] = (SV, sv_sdb)

    acc_n = [0]
    dsec_n = [0]
    sv_sdb1 = sv_sdb
    h_sv = {}
    h_done = {}
    hu = 0

    def d_section(half, qt):
        nonlocal qwdTb_ss, sv_sdb1
        src_w = qwdTa if half == 0 else qwdTb
        need_ss = qwdTa_ss if half == 0 else qwdTb_ss
        u = dsec_n[0] % 2
        dsec_n[0] += 1
        pname = "pspD0" if u == 0 else "pspD1"
        ppslice = psp[:, u * 512:(u + 1) * 512]
        row = qt // 4
        fine = (half == 0 and qt % 4 == 0)
        hw0 = (SS, h_fine[(row, 0)]) if fine else (SS, h_done[row])
        w = ps_w(pname) + [need_ss, hw0]
        if pname not in pscons and "psp" in pscons:
            w = w + [pscons["psp"]]
        for ic in range(IC):
            mmw = w if ic == 0 else (
                [(SS, h_fine[(row, ic)])] if fine else [])
            stv = sch.op("tensor",
                lambda e, ppslice=ppslice, ic=ic, qt=qt, src_w=src_w: e.matmul(
                    ppslice, g2[:, ic * 2048 + qt * 128:ic * 2048 + (qt + 1) * 128],
                    src_w[:, ic * 512:(ic + 1) * 512],
                    start=(ic == 0), stop=(ic == IC - 1)),
                waits=mmw, inc=ST if ic == IC - 1 else None)
        psprod[pname] = (ST, stv)
        wvo = [psprod[pname], (SV, sv_sdb1 if half else sv_sdb)]
        j = acc_n[0] + 1
        if j >= 3:
            wvo.append((SGA, j - 2))
        svo = sch.op("vector", lambda e, ppslice=ppslice, u=u, half=half: e.tensor_tensor(
            otile[:, u * 512:(u + 1) * 512], ppslice,
            sdb[:, half * 512:(half + 1) * 512], ALU.mult),
            waits=wvo, inc=SV)
        pscons[pname] = (SV, svo)
        if j >= 2:
            ckpt("gpsimd", SG, SGA)
        sch.op("gpsimd", lambda e, qt=qt, u=u, half=half: e.dma_start(
            out=dy[qt * 128:(qt + 1) * 128, half * 512:(half + 1) * 512],
            in_=otile[:, u * 512:(u + 1) * 512], accum_op=ALU.add),
            waits=[(SV, svo), (SYW, 16)], inc=SG, inc_n=16)
        acc_n[0] = j
        if half == 0 and qt in (0, 2):
            dc = 6 + qt // 2
            qwdTb_ss = wd_quant(dc, qwdTb, dc - 4)
            if dc == 7:
                stv = sch.op("tensor", lambda e: e.matmul(
                    pst[:, :], onesf[:, :], srow[0:1, 512:1024],
                    start=True, stop=True),
                    waits=ps_w("pst") + [(SV, SV.n)], inc=ST)
                psprod["pst"] = (ST, stv)
                sv_sdb1 = sch.op("vector", lambda e: e.tensor_scalar(
                    sdb[:, 512:1024], pst[:, :], shb[:, :], None, ALU.mult),
                    waits=[psprod["pst"]], inc=SV)
                pscons["pst"] = (SV, sv_sdb1)

    pend = None

    h_fine = {}

    def h_sub(p):
        oc, Gt, hp2, svc = p
        sch.op("scalar", lambda e, oc=oc, Gt=Gt, hp2=hp2: e.activation(
            g2[:, oc * 2048 + Gt * 512:oc * 2048 + (Gt + 1) * 512],
            scr[:, hp2:hp2 + 512], AF.Identity, bias=cNMG[:, :], scale=1.0),
            waits=[(SV, svc)], inc=SS)
        h_fine[(Gt, oc)] = SS.n

    def h_unit(Gt, oc):
        nonlocal hu, pend
        hp2 = (hu % 2) * 512
        ssq = sch.op("scalar", lambda e, oc=oc, Gt=Gt, hp2=hp2: e.activation(
            scr[:, hp2:hp2 + 512], g2[:, oc * 2048 + Gt * 512:oc * 2048 + (Gt + 1) * 512],
            AF.Identity, bias=cMAG[:, :], scale=rhb[:, 0:1]),
            waits=[(SV, svb2)], inc=SS)
        svc = sch.op("vector", lambda e, hp2=hp2: e.tensor_scalar(
            scr[:, hp2:hp2 + 512], scr[:, hp2:hp2 + 512],
            CMAGIC + 127.0, CMAGIC - 128.0, ALU.min, ALU.max),
            waits=[(SS, ssq)], inc=SV)
        if pend is not None:
            h_sub(pend)
        pend = (oc, Gt, hp2, svc)
        hu += 1

    def h_flush(Gt):
        nonlocal pend
        h_sub(pend)
        pend = None
        h_done[Gt] = SS.n

    for oc in range(IC):
        h_unit(0, oc)
    h_flush(0)
    for Gt in range(4):
        nxt = Gt + 1
        k = 0
        for j, qt in enumerate(range(Gt * 4, Gt * 4 + 4)):
            d_section(0, qt)
            if Gt > 0:
                d_section(1, (Gt - 1) * 4 + j)
            if nxt < 4:
                target = IC if j == 3 else (j + 1) * 6
                while k < min(target, IC):
                    h_unit(nxt, k)
                    k += 1
        if nxt < 4:
            h_flush(nxt)
    for qt in range(12, 16):
        d_section(1, qt)

    sch.wait("gpsimd", SG, SG.n)
    sch.wait("sync", SX, SX.n)
    with nc.Block() as block:
        sch.play(block)

    esd.close()
    esc.close()
    es0.close()
    return nc


_CACHE = {}


def kernel(L0_attn_x, L0_attn_w_in, L0_attn_b_in, L0_attn_w_out, L0_attn_b_out,
           L0_ffn_clip_x, L0_ffn_clip_down, L0_ffn_w_gate, L0_ffn_w_up,
           L0_ffn_w_down):
    x = np.ascontiguousarray(np.asarray(L0_attn_x, dtype=np.float32))
    w_in = np.asarray(L0_attn_w_in, dtype=np.float32)
    b_in = np.asarray(L0_attn_b_in, dtype=np.float32)
    w_out = np.asarray(L0_attn_w_out, dtype=np.float32)
    b_out = np.asarray(L0_attn_b_out, dtype=np.float32)
    w_gate = np.ascontiguousarray(np.asarray(L0_ffn_w_gate, dtype=np.float32))
    w_up = np.ascontiguousarray(np.asarray(L0_ffn_w_up, dtype=np.float32))
    w_down = np.ascontiguousarray(np.asarray(L0_ffn_w_down, dtype=np.float32))
    clip_x = np.ascontiguousarray(np.asarray(L0_ffn_clip_x, dtype=np.float32).reshape(1, 1))
    clip_d = np.ascontiguousarray(np.asarray(L0_ffn_clip_down, dtype=np.float32).reshape(1, 1))

    w_inT = np.ascontiguousarray(w_in.T)
    w_outT = np.ascontiguousarray(w_out.T)
    b_pm = np.ascontiguousarray(b_in.reshape(24, 128).T)
    b_v_row = np.ascontiguousarray(b_in[2048:3072].reshape(1, 1024))
    b_out_row = np.ascontiguousarray(b_out.reshape(1, 1024))

    r = np.arange(128)[:, None]
    c = np.arange(256)[None, :]
    band_ok = (c - r >= 0) & (c - r <= 128)
    mask_band = np.where(band_ok, 0.0, NEG).astype(ml_dtypes.bfloat16)
    mask_first_h0 = np.where(band_ok & (c >= 128), 0.0, NEG).astype(ml_dtypes.bfloat16)
    ident_bf = np.eye(128, dtype=np.float32).astype(ml_dtypes.bfloat16)
    ident_f32 = np.eye(128, dtype=np.float32)
    ones_bf = np.ones((1, 128), np.float32).astype(ml_dtypes.bfloat16)
    ones_f32 = np.ones((1, 128), np.float32)
    zero_halo = np.zeros((128, 1024), np.float32)

    if "nc" not in _CACHE:
        _CACHE["nc"] = build_nc()
    nc = _CACHE["nc"]

    in_maps = []
    for core in range(8):
        b, h = core // 2, core % 2
        xm = np.ascontiguousarray(x[b, h * 2048:(h + 1) * 2048])
        xhl = zero_halo if h == 0 else np.ascontiguousarray(x[b, h * 2048 - 128:h * 2048])
        in_maps.append({
            "x_main": xm, "x_halo": xhl,
            "mask_first": mask_first_h0 if h == 0 else mask_band,
            "mask_band": mask_band,
            "w_inT": w_inT, "w_outT": w_outT, "b_pm": b_pm,
            "b_v_row": b_v_row, "b_out_row": b_out_row,
            "w_gate": w_gate, "w_up": w_up, "w_down": w_down,
            "clip_x": clip_x, "clip_down": clip_d,
            "ident_bf": ident_bf, "ident_f32": ident_f32,
            "ones_bf": ones_bf, "ones_f32": ones_f32,
        })

    res = run_bass_kernel_spmd(nc, in_maps, core_ids=list(range(8)))
    out = np.empty((4, 4096, 1024), dtype=np.float32)
    for core in range(8):
        b, h = core // 2, core % 2
        out[b, h * 2048:(h + 1) * 2048] = res.results[core]["y"]
    return out
